# version 04
# baseline (speedup 1.0000x reference)
import numpy as np
import ml_dtypes
from contextlib import ExitStack

import concourse.bass as bass
import concourse.mybir as mybir
from concourse.bass_utils import run_bass_kernel_spmd

F32 = mybir.dt.float32
BF16 = mybir.dt.bfloat16
AF = mybir.ActivationFunctionType
ALU = mybir.AluOpType
AX = mybir.AxisListType.X

T = 2048
D = 1024
E = 2048
NST = 64
ET = 16
KD = 8
NCH = 4
TC = 512
NE = 8
DE = 512
TT = 16
NG = 4

DBG = []
PHASES = 5
P4SUB = 9
N_CORES = 8


class Buf:
    __slots__ = ("ap", "w", "r", "name")

    def __init__(self, ap, name=""):
        self.ap = ap
        self.w = None
        self.r = {}
        self.name = name


class EM:
    ENGS = ("v", "s", "t", "g", "q")

    def __init__(self, nc, ctx):
        self.nc = nc
        self.plan = {e: [] for e in self.ENGS}
        self.sems = {k: ctx.enter_context(nc.semaphore(f"em_{k}"))
                     for k in ("v", "s", "t", "g", "qd", "gd", "qp", "gp")}
        self.count = {k: 0 for k in self.sems}
        self.mark = {e: {} for e in self.ENGS}
        self.pub_covered = {"qd": 0, "gd": 0}

    def _wait(self, eng, key, cnt):
        if cnt <= 0:
            return
        if key in ("qd", "gd"):
            issuer = "q" if key == "qd" else "g"
            pubkey = "qp" if key == "qd" else "gp"
            if eng == issuer:
                cnt = self.count[key]
            else:
                if self.pub_covered[key] < cnt:
                    total = self.count[key]
                    self._wait(issuer, key, total)
                    self.count[pubkey] += 1
                    sem = self.sems[pubkey]
                    self.plan[issuer].append(
                        lambda e, sem=sem: e.nop().then_inc(sem, 1))
                    self.pub_covered[key] = total
                self._wait(eng, pubkey, self.count[pubkey])
                return
        if self.mark[eng].get(key, 0) >= cnt:
            return
        self.mark[eng][key] = cnt
        sem = self.sems[key]
        val = cnt * 16 if key in ("qd", "gd") else cnt
        self.plan[eng].append(lambda e, sem=sem, val=val: e.wait_ge(sem, val))

    def emit(self, eng, fn, r=(), w=(), inc=True, dma=False, wfirst=()):
        if eng == "q":
            semkey, dma = "qd", True
        elif eng == "g" and dma:
            semkey = "gd"
        else:
            semkey = eng
        for b in r:
            if b.w is not None:
                self._wait(eng, *b.w)
        for b in list(w) + list(wfirst):
            if b.w is not None:
                self._wait(eng, *b.w)
            for key, cnt in b.r.items():
                self._wait(eng, key, cnt)
        if inc:
            self.count[semkey] += 1
            cnt = self.count[semkey]
            sem = self.sems[semkey]
            amt = 16 if semkey in ("qd", "gd") else 1
            self.plan[eng].append(
                lambda e, fn=fn, sem=sem, amt=amt: fn(e).then_inc(sem, amt))
        else:
            cnt = self.count[semkey] + 1
            self.plan[eng].append(lambda e, fn=fn: fn(e))
        for b in r:
            if b.r.get(semkey, 0) < cnt:
                b.r[semkey] = cnt
        for b in w:
            b.w = (semkey, cnt)
            b.r = {}
        return cnt

    def barrier(self):
        for eng in self.ENGS:
            for key in self.sems:
                self._wait(eng, key, self.count[key])

    def replay(self, block):
        block.vector(lambda e: [f(e) for f in self.plan["v"]])
        block.scalar(lambda e: [f(e) for f in self.plan["s"]])
        block.tensor(lambda e: [f(e) for f in self.plan["t"]])
        block.gpsimd(lambda e: [f(e) for f in self.plan["g"]])
        block.sync(lambda e: [f(e) for f in self.plan["q"]])


class KB:
    def __init__(self, dbg=()):
        self.nc = bass.Bass("TRN2", target_bir_lowering=False)
        self.ctx = ExitStack()
        self.em = EM(self.nc, self.ctx)
        self.dbg_names = set(dbg)
        self._uid = 0

    def sb(self, name, shape, dtype):
        h = self.ctx.enter_context(self.nc.sbuf_tensor(name, shape, dtype))
        return Buf(h.ap(), name)

    def dram(self, name, shape, dtype, kind="Internal"):
        return Buf(self.nc.dram_tensor(name, shape, dtype, kind=kind).ap(), name)

    def view(self, buf, ap, name=""):
        return Buf(ap, name or buf.name + "/v")

    def dump(self, name, bufs, width=T, dtype=F32):
        if name not in self.dbg_names:
            return
        d = self.dram("dbg_" + name, [len(bufs), 128, width], dtype,
                      kind="ExternalOutput")
        for i, b in enumerate(bufs):
            self.em.emit("q", lambda e, i=i, b=b: e.dma_start(
                d.ap[i], b.ap[:, :width]), r=[b], w=[d])

    def build(self):
        em = self.em
        self.i_x = self.dram("x", [T, D], F32, kind="ExternalInput")
        self.i_inw = self.dram("inw", [D, 2 * E], F32, kind="ExternalInput")
        self.i_convw = self.dram("convw", [4, E], F32, kind="ExternalInput")
        self.i_convb = self.dram("convb", [E], F32, kind="ExternalInput")
        self.i_dtw = self.dram("dtw", [E, E], F32, kind="ExternalInput")
        self.i_dtb = self.dram("dtb", [E], F32, kind="ExternalInput")
        self.i_bcw = self.dram("bcw", [E, 2 * NST], F32, kind="ExternalInput")
        self.i_alog = self.dram("alog", [1, NST], F32, kind="ExternalInput")
        self.i_opw = self.dram("opw", [E, D], F32, kind="ExternalInput")
        self.i_gw = self.dram("gw", [D, E], F32, kind="ExternalInput")
        self.i_uw = self.dram("uw", [D, E], F32, kind="ExternalInput")
        self.i_dw = self.dram("dw", [E, D], F32, kind="ExternalInput")
        self.i_normw = self.dram("normw", [D], F32, kind="ExternalInput")
        self.i_fnormw = self.dram("fnormw", [D], F32, kind="ExternalInput")
        self.i_w1 = self.dram("w1", [NE, D, DE], F32, kind="ExternalInput")
        self.i_b1 = self.dram("b1", [NE, DE], F32, kind="ExternalInput")
        self.i_w2 = self.dram("w2", [NE, DE, D], F32, kind="ExternalInput")
        self.i_b2 = self.dram("b2", [NE, D], F32, kind="ExternalInput")
        self.i_cent = self.dram("cent", [NE, D], F32, kind="ExternalInput")
        self.i_wroute = self.dram("wroute", [NE, D], F32, kind="ExternalInput")
        self.o_out = self.dram("out", [T, D], F32, kind="ExternalOutput")

        self.d_dt = self.dram("d_dt", [ET, 128, T], F32)
        self.d_u = self.dram("d_u", [ET, 128, T], BF16)
        self.d_z = self.dram("d_z", [ET, 128, T], BF16)
        self.d_xf = self.dram("d_xf", [KD, 128, T], BF16)
        self.d_xp = self.dram("d_xp", [KD, 128, T], BF16)
        self.d_moe = self.dram("d_moe", [KD, 128, T], BF16)
        self.d_opw16 = self.dram("d_opw16", [ET, KD, 128, 128], BF16)
        self.d_w116 = self.dram("d_w116", [NE, KD, 128, DE], BF16)
        self.d_w216 = self.dram("d_w216", [NE, 4, 128, D], BF16)
        self.d_gud16 = self.dram("d_gud16", [2, KD, 128, E], BF16)
        self.d_dw16 = self.dram("d_dw16", [ET, 128, D], BF16)
        self.d_bc = self.dram("d_bc", [128, T], BF16)
        self.d_g = self.dram("d_g", [NE, T], BF16)

        self.F = [self.sb(f"fa{i}", [128, 2048], F32) for i in range(5)]
        self.B = [self.sb(f"ba{i}", [128, 2052], BF16) for i in range(26)]
        self.xn = self.B[0:8]

        self.ones16 = self.sb("ones16", [1, 128], BF16)
        self.ones32 = self.sb("ones32", [1, 128], F32)
        self.onecol16 = self.sb("onecol16", [128, 1], BF16)
        self.id16 = self.sb("id16", [128, 128], BF16)
        self.id32 = self.sb("id32", [128, 128], F32)
        self.consts = self.sb("consts", [128, 8], F32)
        self.scol = self.sb("scol", [128, 512], F32)
        self.wcol = self.sb("wcol", [128, KD], F32)
        self.fwcol = self.sb("fwcol", [128, KD], F32)
        self.cbcol = self.sb("cbcol", [128, ET], F32)
        self.dbcol = self.sb("dbcol", [128, ET], F32)
        self.cwcol = self.sb("cwcol", [128, ET * 4], F32)
        self.b1col = self.sb("b1col", [128, NE * 4], F32)
        self.asb = self.sb("asb", [128, NST], F32)
        self.arow = self.sb("arow", [1, NST], F32)
        self.bcsb = self.sb("bcsb", [128, T], BF16)
        self.gatesf = self.sb("gatesf", [NE, T], BF16)
        self.b2sb = self.sb("b2sb", [NE, D], BF16)
        self.hstate = [self.sb(f"hst{i}", [128, ET * NST], F32) for i in range(2)]
        self.rowT = self.sb("rowT", [TT, 128], F32)
        self.brow = [self.sb(f"brow{i}", [1, 2 * NG * TC], BF16) for i in range(2)]
        self.grow = self.sb("grow", [1, NE * TC], BF16)
        self.srow = self.sb("srow", [1, 3 * 1024], F32)
        self.rowflat = Buf(self.srow.ap[:, :2048], "rowflat")
        self.wf32 = [self.sb(f"wf{i}", [128, 128], F32) for i in range(2)]
        self.wb16 = [self.sb(f"wb{i}", [128, 128], BF16) for i in range(8)]

        PA = self.ctx.enter_context(self.nc.psum_tensor("PA", [128, 2048], F32))
        PB = self.ctx.enter_context(self.nc.psum_tensor("PB", [128, 2048], F32))
        self.PA = Buf(PA.ap(), "PA")
        self.PB = Buf(PB.ap(), "PB")

        self._p0_consts()
        if PHASES >= 1:
            self._p1_load_ln()
            em.barrier()
        if PHASES >= 2:
            self._p2_mixer_pre()
            em.barrier()
        if PHASES >= 3:
            self._p25_routing()
            em.barrier()
        if PHASES >= 4:
            self._p3_scan()
            em.barrier()
        if PHASES >= 5:
            self._p4_ffn_final()
            em.barrier()
        else:
            for tt in range(TT):
                em.emit("q", lambda e, i=tt: e.dma_start(
                    self.o_out.ap[i * 128:(i + 1) * 128, :],
                    self.F[0].ap[:, :D]), r=[self.F[0]], w=[self.o_out])
            em.barrier()

        for nm in list(self.dbg_names):
            if not nm.startswith("d_"):
                continue
            t = getattr(self, nm, None)
            if t is None:
                continue
            sh = list(t.ap.shape)
            o = self.dram("dbgo_" + nm, sh, t.ap.dtype, kind="ExternalOutput")
            em.emit("q", lambda e, o=o, t=t: e.dma_start(o.ap, t.ap),
                    r=[t], w=[o])
        em.barrier()

        block = self.ctx.enter_context(self.nc.Block())
        em.replay(block)
        self.ctx.close()
        return self.nc

    def _p0_consts(self):
        em = self.em
        for b, val in ((self.ones16, 1.0), (self.ones32, 1.0), (self.onecol16, 1.0)):
            em.emit("g", lambda g, b=b, v=val: g.memset(b.ap, v), w=[b])
        for ident in (self.id16, self.id32):
            em.emit("g", lambda g, a=ident.ap: g.memset(a, 0.0), w=[ident])
            em.emit("g", lambda g, a=ident.ap: g.affine_select(
                out=a, in_=a, compare_op=ALU.not_equal, fill=1.0,
                base=0, pattern=[[-1, 128]], channel_multiplier=1),
                r=[ident], w=[ident])
        for col, val in ((0, 1e-5), (1, 1.0 / D), (2, -0.5), (3, 1.5), (4, 1e-12)):
            em.emit("g", lambda g, c=col, v=val: g.memset(
                self.consts.ap[:, c:c + 1], v), w=[self.consts])

        def vload(dst, src_ap, ncols):
            v2 = src_ap.rearrange("(a b) -> a b", b=128).transpose([1, 0])
            em.emit("q", lambda e, d=dst, s=v2, n=ncols: e.dma_start(
                d.ap[:, :n], s, allow_slow_non_contiguous=True), w=[dst])

        vload(self.wcol, self.i_normw.ap, KD)
        vload(self.fwcol, self.i_fnormw.ap, KD)
        vload(self.cbcol, self.i_convb.ap, ET)
        vload(self.dbcol, self.i_dtb.ap, ET)
        for wd in range(4):
            cv = self.i_convw.ap[wd].rearrange("(i b) -> b i", b=128)
            cwd = self.cwcol.ap.rearrange("p (i w) -> p i w", w=4)[:, :, wd]
            em.emit("q", lambda e, cv=cv, cwd=cwd: e.dma_start(cwd, cv,
                    allow_slow_non_contiguous=True), w=[self.cwcol])
        for e_ in range(NE):
            b1v = self.i_b1.ap[e_].rearrange("(m b) -> b m", b=128)
            b1d = self.b1col.ap[:, e_ * 4:(e_ + 1) * 4]
            em.emit("q", lambda e, b1v=b1v, b1d=b1d: e.dma_start(b1d, b1v,
                    allow_slow_non_contiguous=True), w=[self.b1col])
        em.emit("q", lambda e: e.dma_start(self.arow.ap, self.i_alog.ap), w=[self.arow])
        em.emit("s", lambda s: s.activation(self.arow.ap, self.arow.ap, AF.Exp),
                r=[self.arow], w=[self.arow])
        em.emit("v", lambda v: v.tensor_scalar(self.arow.ap, self.arow.ap, -1.0,
                None, ALU.mult), r=[self.arow], w=[self.arow])
        em.emit("t", lambda t: t.matmul(self.PA.ap[:, :NST], self.ones32.ap,
                self.arow.ap, start=True, stop=True),
                r=[self.ones32, self.arow], w=[self.PA])
        em.emit("s", lambda s: s.copy(self.asb.ap, self.PA.ap[:, :NST]),
                r=[self.PA], w=[self.asb])
        b2f = self.F[4]
        em.emit("q", lambda e: e.dma_start(b2f.ap[:NE, :D], self.i_b2.ap), w=[b2f])
        em.emit("v", lambda v: v.tensor_copy(self.b2sb.ap, b2f.ap[:NE, :D]),
                r=[b2f], w=[self.b2sb])

    def _rsqrt(self, src_ap, dst_ap, rbufs, wbufs, scr_ap, scrbuf, eps_col=0):
        em = self.em
        n = src_ap.shape[-1]
        em.emit("s", lambda s: s.activation(scr_ap[:, :n], src_ap, AF.Sqrt,
                bias=self.consts.ap[:src_ap.shape[0], eps_col:eps_col + 1]),
                r=rbufs + [self.consts], w=[scrbuf])
        em.emit("v", lambda v: v.reciprocal(dst_ap, scr_ap[:, :n]),
                r=[scrbuf], w=wbufs)

    def _wtile(self, src_ap, idx, scale_col=None):
        em = self.em
        f = self.wf32[idx % 2]
        b = self.wb16[idx % 8]
        em.emit("g", lambda e, s=src_ap, d=f.ap: e.dma_start(d, s), w=[f], dma=True)
        if scale_col is None:
            em.emit("v", lambda v, s=f.ap, d=b.ap: v.tensor_copy(d, s), r=[f], w=[b])
        else:
            col, cbuf = scale_col
            em.emit("v", lambda v, s=f.ap, d=b.ap, c=col: v.tensor_scalar(
                d, s, c, None, ALU.mult), r=[f, cbuf], w=[b])
        return b

    def _bcast_row(self, src_row_ap, rbufs, dst_ap, wbuf, psum, width, f32=False):
        em = self.em
        ones = self.ones32 if f32 else self.ones16
        em.emit("t", lambda t: t.matmul(psum.ap[:, :width], ones.ap, src_row_ap,
                start=True, stop=True), r=[ones] + rbufs, w=[psum])
        em.emit("s", lambda s: s.copy(dst_ap, psum.ap[:, :width]),
                r=[psum], w=[wbuf])

    def _p1_load_ln(self):
        em = self.em
        sums = self.view(self.scol, self.scol.ap[:, 0:32], "lnsums")
        mcol = self.view(self.scol, self.scol.ap[:, 32:48], "mcol")
        varc = self.view(self.scol, self.scol.ap[:, 48:64], "varc")
        rstd = self.view(self.scol, self.scol.ap[:, 64:80], "rstd")
        nsc = self.view(self.scol, self.scol.ap[:, 80:112], "nsc")
        rmss = self.view(self.scol, self.scol.ap[:, 112:128], "rmss")
        self.rms_sumsq = rmss
        self.ln_cols = (mcol, rstd)
        scrt = self.F[4]
        xf = self.B[8:16]
        for tt in range(TT):
            sl = self.F[tt % 4]
            em.emit("q", lambda e, d=sl, i=tt: e.dma_start(
                d.ap[:, :D], self.i_x.ap[i * 128:(i + 1) * 128, :]), w=[sl])
            em.emit("s", lambda s, x=sl, i=tt: s.activation(
                scrt.ap[:, :D], x.ap[:, :D], AF.Copy,
                accum_out=sums.ap[:, i:i + 1]), r=[sl], w=[scrt, sums])
            em.emit("s", lambda s, x=sl, i=tt: s.activation(
                scrt.ap[:, :D], x.ap[:, :D], AF.Square,
                accum_out=sums.ap[:, 16 + i:17 + i]), r=[sl], w=[scrt, sums])
            em.emit("v", lambda v, i=tt: v.tensor_scalar(
                mcol.ap[:, i:i + 1], sums.ap[:, i:i + 1], 1.0 / D, None,
                ALU.mult), r=[sums], w=[mcol])
            em.emit("v", lambda v, i=tt: v.tensor_tensor(
                varc.ap[:, i:i + 1], mcol.ap[:, i:i + 1], mcol.ap[:, i:i + 1],
                ALU.mult), r=[mcol], w=[varc])
            em.emit("v", lambda v, i=tt: v.scalar_tensor_tensor(
                varc.ap[:, i:i + 1], sums.ap[:, 16 + i:17 + i], 1.0 / D,
                varc.ap[:, i:i + 1], ALU.mult, ALU.subtract),
                r=[sums, varc], w=[varc])
            self._rsqrt(varc.ap[:, tt:tt + 1], rstd.ap[:, tt:tt + 1],
                        [varc], [rstd], nsc.ap[:, 0:2], nsc)
            self._transpose_tile(sl, tt, xf, F32)
            em.emit("v", lambda v, x=sl, i=tt: v.tensor_scalar(
                x.ap[:, :D], x.ap[:, :D], mcol.ap[:, i:i + 1],
                rstd.ap[:, i:i + 1], ALU.subtract, ALU.mult),
                r=[sl, mcol, rstd], w=[sl])
            em.emit("s", lambda s, x=sl, i=tt: s.activation(
                scrt.ap[:, :D], x.ap[:, :D], AF.Square,
                accum_out=rmss.ap[:, i:i + 1]), r=[sl], w=[scrt, rmss])
            self._transpose_tile(sl, tt, self.xn, F32)
        for k in range(KD):
            em.emit("q", lambda e, i=k, b=xf[k]: e.dma_start(
                self.d_xf.ap[i], b.ap[:, :T]), r=[xf[k]], w=[self.d_xf])
        self.dump("xn", self.xn, T, BF16)

    def _transpose_tile(self, tile, tt, fm_tiles, in_dtype):
        em = self.em
        ident = self.id32 if in_dtype == F32 else self.id16
        for kq in range(KD // 4):
            p = self.PA if kq % 2 == 0 else self.PB
            for j in range(4):
                k = kq * 4 + j
                em.emit("t", lambda t, x=tile, kk=k, pp=p, jj=j: t.transpose(
                    pp.ap[:, jj * 128:(jj + 1) * 128],
                    x.ap[:, kk * 128:(kk + 1) * 128], ident.ap),
                    r=[tile, ident], w=([p] if j == 3 else []),
                    wfirst=([p] if j == 0 else []), inc=(j == 3))
            for j in range(4):
                k = kq * 4 + j
                em.emit("s", lambda s, kk=k, pp=p, jj=j, i=tt: s.copy(
                    fm_tiles[kk].ap[:, i * 128:(i + 1) * 128],
                    pp.ap[:, jj * 128:(jj + 1) * 128]), r=[p], w=[fm_tiles[k]])

    def _p2_mixer_pre(self):
        em = self.em
        rmsr = self.view(self.scol, self.scol.ap[:, 128:144], "rmsr")
        nsc = self.view(self.scol, self.scol.ap[:, 144:176], "nsc2")
        mean = self.view(self.scol, self.scol.ap[:, 176:192], "rmsmean")
        em.emit("v", lambda v: v.tensor_scalar(mean.ap, self.rms_sumsq.ap,
                1.0 / D, None, ALU.mult), r=[self.rms_sumsq], w=[mean])
        self._rsqrt(mean.ap, rmsr.ap, [mean], [rmsr], nsc.ap, nsc)
        rb = self.F[4]
        self._bcast_cols(rmsr, rb)
        self.dump("rmsr", [rb], T, BF16)

        xz = self.B[8:10]
        xc = self.B[10:26]
        zbuf = self.B[9]
        wi = 0
        ztmp = self.F[2]
        for m in range(32):
            p = self.PA if m % 2 == 0 else self.PB
            order_m = m + 16 if m < 16 else m - 16
            for k in range(KD):
                wb = self._wtile(self.i_inw.ap[k * 128:(k + 1) * 128,
                                               order_m * 128:(order_m + 1) * 128],
                                 wi, scale_col=(self.wcol.ap[:, k:k + 1], self.wcol))
                wi += 1
                for f in range(4):
                    em.emit("t", lambda t, pp=p, w=wb, kk=k, ff=f: t.matmul(
                        pp.ap[:, ff * 512:(ff + 1) * 512], w.ap,
                        self.xn[kk].ap[:, ff * 512:(ff + 1) * 512],
                        start=(kk == 0), stop=(kk == KD - 1)),
                        r=[wb, self.xn[k]],
                        w=([p] if (k == KD - 1 and f == 3) else []),
                        wfirst=([p] if (k == 0 and f == 0) else []),
                        inc=True)
            if order_m >= ET:
                i = order_m - ET
                em.emit("v", lambda v, pp=p: v.tensor_tensor(
                    ztmp.ap, pp.ap, rb.ap[:, :T], ALU.mult), r=[p, rb], w=[ztmp])
                em.emit("s", lambda s: s.activation(zbuf.ap[:, :T], ztmp.ap,
                        AF.Silu), r=[ztmp], w=[zbuf])
                em.emit("q", lambda e, i=i: e.dma_start(self.d_z.ap[i],
                        zbuf.ap[:, :T]), r=[zbuf], w=[self.d_z])
            else:
                i = order_m
                sl = xz[i % 2]
                em.emit("g", lambda g, b=sl: g.memset(b.ap[:, 0:1], 0.0), w=[sl])
                em.emit("g", lambda g, b=sl: g.memset(b.ap[:, T + 1:T + 3], 0.0),
                        r=[sl], w=[sl])
                em.emit("v", lambda v, pp=p, b=sl: v.tensor_tensor(
                    b.ap[:, 1:T + 1], pp.ap, rb.ap[:, :T], ALU.mult),
                    r=[p, rb], w=[sl])
                acc = [self.F[0], self.F[1]]
                em.emit("v", lambda v, i=i, a=acc[0], b=sl: v.tensor_scalar(
                    a.ap, b.ap[:, 0:T], self.cwcol.ap[:, i * 4:i * 4 + 1], None,
                    ALU.mult), r=[sl, self.cwcol], w=[acc[0]])
                for wd in range(1, 4):
                    src_, dst_ = acc[(wd + 1) % 2], acc[wd % 2]
                    em.emit("v", lambda v, i=i, s_=src_, d_=dst_, b=sl, wd=wd:
                            v.scalar_tensor_tensor(
                        d_.ap, b.ap[:, wd:wd + T],
                        self.cwcol.ap[:, i * 4 + wd:i * 4 + wd + 1], s_.ap,
                        ALU.mult, ALU.add), r=[sl, self.cwcol, src_], w=[dst_])
                em.emit("s", lambda s, i=i, a=acc[1]: s.activation(
                    xc[i].ap[:, :T], a.ap, AF.Silu,
                    bias=self.cbcol.ap[:, i:i + 1]), r=[acc[1], self.cbcol], w=[xc[i]])
        self.dump("xc", xc, T, BF16)
        self.dump("xzlast", xz, 2051, BF16)
        self.dump("cwcol", [self.cwcol], 64, F32)
        self.dump("cbcol", [self.cbcol], ET, F32)

        sp1 = self.F[2]
        dtf = [self.F[3], self.F[3]]
        for m in range(ET):
            p = self.PA if m % 2 == 0 else self.PB
            for k in range(ET):
                wb = self._wtile(self.i_dtw.ap[k * 128:(k + 1) * 128,
                                               m * 128:(m + 1) * 128], wi)
                wi += 1
                for f in range(4):
                    em.emit("t", lambda t, pp=p, w=wb, kk=k, ff=f: t.matmul(
                        pp.ap[:, ff * 512:(ff + 1) * 512], w.ap,
                        xc[kk].ap[:, ff * 512:(ff + 1) * 512],
                        start=(kk == 0), stop=(kk == ET - 1)),
                        r=[wb, xc[k]],
                        w=([p] if (k == ET - 1 and f == 3) else []),
                        wfirst=([p] if (k == 0 and f == 0) else []),
                        inc=True)
            d = dtf[m % 2]
            sp2 = self.F[4]
            em.emit("s", lambda s, pp=p, i=m: s.activation(
                sp1.ap, pp.ap, AF.Exp, bias=self.dbcol.ap[:, i:i + 1]),
                r=[p, self.dbcol], w=[sp1])
            em.emit("s", lambda s, sp2=sp2: s.activation(sp2.ap, sp1.ap, AF.Ln,
                    bias=1.0), r=[sp1], w=[sp2])
            em.emit("v", lambda v, d=d, sp2=sp2: v.tensor_scalar(
                d.ap, sp2.ap, 0.001, 0.1, ALU.max, ALU.min), r=[sp2], w=[d])
            em.emit("q", lambda e, d=d, i=m: e.dma_start(self.d_dt.ap[i], d.ap),
                    r=[d], w=[self.d_dt])
            ub = zbuf
            em.emit("v", lambda v, d=d, u=ub, i=m: v.tensor_tensor(
                u.ap[:, :T], d.ap, xc[i].ap[:, :T], ALU.mult),
                r=[d, xc[m]], w=[ub])
            em.emit("q", lambda e, u=ub, i=m: e.dma_start(self.d_u.ap[i],
                    u.ap[:, :T]), r=[ub], w=[self.d_u])
        if "sp" in self.dbg_names:
            pm15 = self.PB
            em.emit("v", lambda v: v.tensor_copy(self.F[1].ap, pm15.ap),
                    r=[pm15], w=[self.F[1]])
        self.dump("dt", dtf, T, F32)
        self.dump("dbcol", [self.dbcol], ET, F32)
        self.dump("sp", [self.F[2], self.F[4], self.F[1]], T, F32)

        p = self.PA
        for k in range(ET):
            wb = self._wtile(self.i_bcw.ap[k * 128:(k + 1) * 128, :], wi)
            wi += 1
            for f in range(4):
                em.emit("t", lambda t, pp=p, w=wb, kk=k, ff=f: t.matmul(
                    pp.ap[:, ff * 512:(ff + 1) * 512], w.ap,
                    xc[kk].ap[:, ff * 512:(ff + 1) * 512],
                    start=(kk == 0), stop=(kk == ET - 1)),
                    r=[wb, xc[k]],
                    w=([p] if (k == ET - 1 and f == 3) else []),
                    wfirst=([p] if (k == 0 and f == 0) else []),
                    inc=True)
        em.emit("s", lambda s: s.copy(self.bcsb.ap, p.ap), r=[p], w=[self.bcsb])
        em.emit("q", lambda e: e.dma_start(self.d_bc.ap, self.bcsb.ap),
                r=[self.bcsb], w=[self.d_bc])
        self.dump("bc", [self.bcsb], T, BF16)

        em.barrier()
        self._precast_weights()

    def _bcast_cols(self, colbuf, out, half=False):
        em = self.em
        p = self.PA
        em.emit("t", lambda t: t.transpose(p.ap[:TT, :128], colbuf.ap,
                self.id32.ap), r=[colbuf, self.id32], w=[p])
        em.emit("s", lambda s: s.copy(self.rowT.ap, p.ap[:TT, :128]),
                r=[p], w=[self.rowT])
        rf = self.rowflat
        em.emit("q", lambda e: e.dma_start(
            rf.ap.rearrange("p (a b) -> p a b", b=128)[:, :TT, :],
            self.rowT.ap), r=[self.rowT], w=[rf])
        p2 = self.PB
        for tq in range(4):
            em.emit("t", lambda t, q=tq, pp=p2: t.matmul(
                pp.ap[:, :512], self.ones32.ap,
                rf.ap[0:1, q * 512:(q + 1) * 512], start=True, stop=True),
                r=[self.ones32, rf], w=[p2])
            em.emit("s", lambda s, pp=p2, q=tq: s.copy(
                out.ap[:, q * 512:(q + 1) * 512], pp.ap[:, :512]),
                r=[p2], w=[out])

    def _precast_weights(self):
        em = self.em
        st = [self.view(self.F[2], self.F[2].ap[:, :1024], "pcs0"),
              self.view(self.F[2], self.F[2].ap[:, 1024:2048], "pcs1")]
        ob = [self.view(self.B[8], self.B[8].ap[:, :1024], "pcb0"),
              self.view(self.B[9], self.B[9].ap[:, :1024], "pcb1")]
        idx = 0

        def cast_store(src_ap, dst_ap, dstbuf, width, scale=None, reshape_c=None):
            nonlocal idx
            s_ = st[idx % 2]
            o_ = ob[idx % 2]
            idx += 1
            em.emit("g", lambda e, s=src_ap, d=s_: e.dma_start(
                d.ap[:, :width], s), w=[s_], dma=True)
            if scale is None:
                em.emit("v", lambda v, s=s_, o=o_: v.tensor_copy(
                    o.ap[:, :width], s.ap[:, :width]), r=[s_], w=[o_])
            else:
                col, cbuf = scale
                em.emit("v", lambda v, s=s_, o=o_, c=col: v.tensor_scalar(
                    o.ap[:, :width], s.ap[:, :width], c, None, ALU.mult),
                    r=[s_, cbuf], w=[o_])
            src_out = o_.ap[:, :width]
            if reshape_c is not None:
                src_out = src_out.rearrange("p (a c) -> p a c", c=reshape_c)
            em.emit("q", lambda e, d=dst_ap, s=src_out: e.dma_start(
                d, s), r=[o_], w=[dstbuf])

        for k in range(ET):
            dst = self.d_opw16.ap[k].rearrange("a p c -> p a c")
            cast_store(self.i_opw.ap[k * 128:(k + 1) * 128, :], dst,
                       self.d_opw16, D, reshape_c=128)
        for e_ in range(NE):
            for k in range(KD):
                cast_store(self.i_w1.ap[e_, k * 128:(k + 1) * 128, :],
                           self.d_w116.ap[e_, k], self.d_w116, DE)
            for k in range(4):
                cast_store(self.i_w2.ap[e_, k * 128:(k + 1) * 128, :],
                           self.d_w216.ap[e_, k], self.d_w216, D)
        for gi, wsrc in ((0, self.i_gw), (1, self.i_uw)):
            for k in range(KD):
                for h in range(2):
                    cast_store(wsrc.ap[k * 128:(k + 1) * 128,
                                       h * 1024:(h + 1) * 1024],
                               self.d_gud16.ap[gi, k][:, h * 1024:(h + 1) * 1024],
                               self.d_gud16, 1024,
                               scale=(self.fwcol.ap[:, k:k + 1], self.fwcol))
        for k in range(ET):
            cast_store(self.i_dw.ap[k * 128:(k + 1) * 128, :],
                       self.d_dw16.ap[k], self.d_dw16, D)

    def _p25_routing(self):
        em = self.em
        cf, wrf, cnb = self.F[0], self.F[1], self.F[2]
        em.emit("q", lambda e: e.dma_start(cf.ap[:NE, :D], self.i_cent.ap), w=[cf])
        em.emit("q", lambda e: e.dma_start(wrf.ap[:NE, :D], self.i_wroute.ap),
                w=[wrf])
        cn2 = self.view(self.scol, self.scol.ap[:NE, 192:196], "cn2")
        scr8 = self.view(self.F[3], self.F[3].ap[:NE, :D], "scr8")
        em.emit("s", lambda s: s.activation(scr8.ap, cf.ap[:NE, :D], AF.Square,
                accum_out=cn2.ap[:, 0:1]), r=[cf], w=[scr8, cn2])
        em.emit("s", lambda s: s.activation(cn2.ap[:, 1:2], cn2.ap[:, 0:1],
                AF.Sqrt), r=[cn2], w=[cn2])
        em.emit("v", lambda v: v.tensor_scalar(cn2.ap[:, 1:2], cn2.ap[:, 1:2],
                self.consts.ap[:NE, 4:5], None, ALU.max),
                r=[cn2, self.consts], w=[cn2])
        em.emit("v", lambda v: v.reciprocal(cn2.ap[:, 2:3], cn2.ap[:, 1:2]),
                r=[cn2], w=[cn2])
        em.emit("v", lambda v: v.tensor_scalar(cnb.ap[:NE, :D], cf.ap[:NE, :D],
                cn2.ap[:, 2:3], None, ALU.mult), r=[cf, cn2], w=[cnb])
        cnT = self.view(self.B[10], self.B[10].ap[:, :KD * NE], "cnT")
        wrT = self.view(self.B[11], self.B[11].ap[:, :KD * NE], "wrT")
        for (srcb, dstb) in ((cnb, cnT), (wrf, wrT)):
            for k in range(KD):
                em.emit("t", lambda t, s=srcb, kk=k: t.transpose(
                    self.PA.ap[:, :NE], s.ap[:NE, kk * 128:(kk + 1) * 128],
                    self.id32.ap[:NE, :NE]), r=[srcb, self.id32], w=[self.PA])
                em.emit("s", lambda s, d=dstb, kk=k: s.copy(
                    d.ap[:, kk * NE:(kk + 1) * NE], self.PA.ap[:, :NE]),
                    r=[self.PA], w=[dstb])
        for (stat, pp) in ((cnT, self.PA), (wrT, self.PB)):
            for f in range(4):
                for k in range(KD):
                    em.emit("t", lambda t, st=stat, kk=k, ff=f, p_=pp: t.matmul(
                        p_.ap[:NE, ff * 512:(ff + 1) * 512],
                        st.ap[:, kk * NE:(kk + 1) * NE],
                        self.xn[kk].ap[:, ff * 512:(ff + 1) * 512],
                        start=(kk == 0), stop=(kk == KD - 1)),
                        r=[stat, self.xn[k]],
                        w=([pp] if (k == KD - 1 and f == 3) else []),
                        wfirst=([pp] if (k == 0 and f == 0) else []),
                        inc=(f == 3))
        csb, rsb = self.F[0], self.F[1]
        em.emit("v", lambda v: v.tensor_copy(csb.ap[:NE, :T],
                self.PA.ap[:NE, :T]), r=[self.PA], w=[csb])
        em.emit("v", lambda v: v.tensor_copy(rsb.ap[:NE, :T],
                self.PB.ap[:NE, :T]), r=[self.PB], w=[rsb])
        costm, routm = self.F[3], self.F[4]
        for (srcb, dstb) in ((csb, costm), (rsb, routm)):
            for tt in range(TT):
                em.emit("t", lambda t, s=srcb, i=tt: t.transpose(
                    self.PA.ap[:, :NE], s.ap[:NE, i * 128:(i + 1) * 128],
                    self.id32.ap[:NE, :NE]), r=[srcb, self.id32], w=[self.PA])
                em.emit("s", lambda s, d=dstb, i=tt: s.copy(
                    d.ap[:, i * NE:(i + 1) * NE], self.PA.ap[:, :NE]),
                    r=[self.PA], w=[dstb])
        tn = self.view(self.scol, self.scol.ap[:, 196:228], "tnorm")
        em.emit("s", lambda s: s.activation(tn.ap[:, 0:TT],
                self.rms_sumsq.ap, AF.Sqrt), r=[self.rms_sumsq], w=[tn])
        em.emit("v", lambda v: v.tensor_scalar(tn.ap[:, 0:TT], tn.ap[:, 0:TT],
                self.consts.ap[:, 4:5], None, ALU.max),
                r=[tn, self.consts], w=[tn])
        em.emit("v", lambda v: v.reciprocal(tn.ap[:, TT:2 * TT],
                tn.ap[:, 0:TT]), r=[tn], w=[tn])
        gtm = self.F[2]
        sc = self.view(self.scol, self.scol.ap[:, 228:240], "tksc")
        msk = self.view(self.scol, self.scol.ap[:, 240:248], "tkmsk")
        for tt in range(TT):
            ls = costm.ap[:, tt * NE:(tt + 1) * NE]
            gs = gtm.ap[:, tt * NE:(tt + 1) * NE]
            em.emit("v", lambda v, ls=ls, i=tt: v.scalar_tensor_tensor(
                ls, ls, tn.ap[:, TT + i:TT + i + 1],
                routm.ap[:, i * NE:(i + 1) * NE], ALU.mult, ALU.add),
                r=[costm, tn, routm], w=[costm])
            em.emit("v", lambda v, ls=ls: v.reduce_max(sc.ap[:, 0:1], ls,
                    axis=AX), r=[costm], w=[sc])
            em.emit("v", lambda v, ls=ls: v.tensor_scalar(
                msk.ap, ls, sc.ap[:, 0:1], None, ALU.is_equal),
                r=[costm, sc], w=[msk])
            em.emit("v", lambda v, ls=ls: v.scalar_tensor_tensor(
                msk.ap, msk.ap, -1e30, ls, ALU.mult, ALU.add),
                r=[msk, costm], w=[msk])
            em.emit("v", lambda v: v.reduce_max(sc.ap[:, 1:2], msk.ap,
                    axis=AX), r=[msk], w=[sc])
            em.emit("v", lambda v: v.tensor_scalar(sc.ap[:, 2:3],
                    sc.ap[:, 0:1], -1.0, None, ALU.mult), r=[sc], w=[sc])
            em.emit("s", lambda s, ls=ls, gs=gs: s.activation(
                gs, ls, AF.Exp, bias=sc.ap[:, 2:3]), r=[costm, sc], w=[gtm])
            em.emit("v", lambda v, ls=ls: v.tensor_scalar(
                msk.ap, ls, sc.ap[:, 1:2], None, ALU.is_ge),
                r=[costm, sc], w=[msk])
            em.emit("v", lambda v, gs=gs: v.tensor_tensor(gs, gs, msk.ap,
                    ALU.mult), r=[gtm, msk], w=[gtm])
            em.emit("v", lambda v, gs=gs: v.reduce_sum(sc.ap[:, 3:4], gs,
                    axis=AX), r=[gtm], w=[sc])
            em.emit("v", lambda v: v.reciprocal(sc.ap[:, 4:5], sc.ap[:, 3:4]),
                    r=[sc], w=[sc])
            em.emit("v", lambda v, gs=gs: v.tensor_scalar(gs, gs,
                    sc.ap[:, 4:5], None, ALU.mult), r=[gtm, sc], w=[gtm])
        for tt in range(TT):
            em.emit("t", lambda t, i=tt: t.transpose(
                self.PA.ap[:NE, :128], gtm.ap[:, i * NE:(i + 1) * NE],
                self.id32.ap), r=[gtm, self.id32], w=[self.PA])
            em.emit("s", lambda s, i=tt: s.copy(
                self.gatesf.ap[:, i * 128:(i + 1) * 128],
                self.PA.ap[:NE, :128]), r=[self.PA], w=[self.gatesf])
        em.emit("q", lambda e: e.dma_start(self.d_g.ap, self.gatesf.ap),
                r=[self.gatesf], w=[self.d_g])
        self.dump("gatesf", [self.view(self.gatesf, self.gatesf.ap, "g")], T, BF16)

    def _p3_scan(self):
        em = self.em
        dA = [self.F[0], self.F[1]]
        dtq = [self.F[2], self.F[3]]
        inp = [self.B[8], self.B[9]]
        hb = [self.B[10], self.B[11]]
        hc = [self.B[12], self.B[13]]
        uq = [self.B[14], self.B[14]]
        y2 = self.B[16:20]
        bbg = [self.B[21], self.B[22]]
        cbg = [self.B[23], self.B[24]]
        heb = [self.B[25], self.B[25]]
        gb = self.view(self.B[20], self.B[20].ap[:, :TC], "gb")
        xpb = self.view(self.B[20], self.B[20].ap[:, 512:1024], "xpb")
        mb = self.view(self.B[20], self.B[20].ap[:, 1024:1536], "mb")
        hetmp = self.view(self.B[20], self.B[20].ap[:, 1536:2048], "hetmp")
        self.moe_tmp = (mb, hetmp)
        ypsum = self.PA
        bcp = [self.view(self.PB, self.PB.ap[:, 0:512], "bcp0"),
               self.view(self.PB, self.PB.ap[:, 512:1024], "bcp1")]
        auxp = [self.view(self.PB, self.PB.ap[:, 1024:1536], "aux0"),
                self.view(self.PB, self.PB.ap[:, 1536:2048], "aux1")]
        for i in range(2):
            em.emit("g", lambda g, b=self.hstate[i], i=i: g.memset(b.ap, 0.0),
                    w=[self.hstate[i]])

        NGRP = NST // NG

        def prebuild(c, ng, bb, cb):
            stg = self.brow[ng % 2]
            s3 = stg.ap.rearrange("p (a b) -> p a b", b=TC)
            em.emit("q", lambda e, n0=ng * NG, cc=c: e.dma_start(
                s3[:, :NG, :],
                self.d_bc.ap[n0:n0 + NG, cc * TC:(cc + 1) * TC]),
                r=[self.d_bc], w=[stg])
            em.emit("q", lambda e, n0=ng * NG, cc=c: e.dma_start(
                s3[:, NG:2 * NG, :],
                self.d_bc.ap[NST + n0:NST + n0 + NG, cc * TC:(cc + 1) * TC]),
                r=[self.d_bc], w=[stg])
            for jn in range(NG):
                for (half, dst, pp) in ((0, bb, bcp[0]), (1, cb, bcp[1])):
                    em.emit("t", lambda t, p_=pp, j_=jn, h=half: t.matmul(
                        p_.ap, self.ones16.ap,
                        s3[0:1, h * NG + j_, :], start=True, stop=True),
                        r=[self.ones16, stg], w=[pp])
                    em.emit("s", lambda s, p_=pp, d=dst, j_=jn: s.copy(
                        d.ap[:, j_ * TC:(j_ + 1) * TC], p_.ap),
                        r=[pp], w=[dst])

        moe_done = {c: 0 for c in range(NCH)}

        for c in range(NCH):
            g3 = self.grow.ap.rearrange("p (a b) -> p a b", b=TC)
            em.emit("q", lambda e, cc=c: e.dma_start(
                g3[:, :, :], self.d_g.ap[:, cc * TC:(cc + 1) * TC]),
                r=[self.d_g], w=[self.grow])
            for d in range(KD):
                pp = auxp[d % 2]
                em.emit("t", lambda t, p_=pp, d_=d, cc=c: t.matmul(
                    p_.ap, self.b2sb.ap[:, d_ * 128:(d_ + 1) * 128],
                    self.gatesf.ap[:, cc * TC:(cc + 1) * TC],
                    start=True, stop=True), r=[self.b2sb, self.gatesf], w=[pp])
                em.emit("s", lambda s, p_=pp: s.copy(mb.ap, p_.ap),
                        r=[pp], w=[mb])
                em.emit("g", lambda e, d_=d, cc=c: e.dma_start(
                    self.d_moe.ap[d_][:, cc * TC:(cc + 1) * TC], mb.ap),
                    r=[mb], w=[self.d_moe], dma=True)
            hs_r = self.hstate[c % 2]
            hs_w = self.hstate[(c + 1) % 2]
            hr3 = hs_r.ap.rearrange("p (e n) -> p e n", n=NST)
            hw3 = hs_w.ap.rearrange("p (e n) -> p e n", n=NST)
            for qd in range(4):
                dtb_ = dtq[qd % 2]
                ub_ = uq[qd % 2]
                for j in range(4):
                    et = qd * 4 + j
                    em.emit("q", lambda e, d=dtb_, i=et, jj=j, cc=c: e.dma_start(
                        d.ap[:, jj * 512:(jj + 1) * 512],
                        self.d_dt.ap[i][:, cc * TC:(cc + 1) * TC]),
                        r=[self.d_dt], w=[dtb_])
                    em.emit("q", lambda e, d=ub_, i=et, jj=j, cc=c: e.dma_start(
                        d.ap[:, jj * 512:(jj + 1) * 512],
                        self.d_u.ap[i][:, cc * TC:(cc + 1) * TC]),
                        r=[self.d_u], w=[ub_])
                prebuild(c, 0, bbg[0], cbg[0])
                for ng in range(NGRP):
                    if ng + 1 < NGRP:
                        prebuild(c, ng + 1, bbg[(ng + 1) % 2], cbg[(ng + 1) % 2])
                    bb = bbg[ng % 2]
                    cb = cbg[ng % 2]
                    for jn in range(NG):
                        n = ng * NG + jn
                        ib = n % 2
                        em.emit("s", lambda s, d=dA[ib], src=dtb_, n_=n: s.activation(
                            d.ap, src.ap, AF.Exp,
                            scale=self.asb.ap[:, n_:n_ + 1]),
                            r=[dtb_, self.asb], w=[dA[ib]])
                        bbr = bb.ap[:, jn * TC:(jn + 1) * TC][:, None, :] \
                            .broadcast_to([128, 4, TC])
                        uv = ub_.ap[:, :2048].rearrange("p (a b) -> p a b", b=TC)
                        iv = inp[ib].ap[:, :2048].rearrange("p (a b) -> p a b", b=TC)
                        em.emit("v", lambda v, iv=iv, uv=uv, bbr=bbr: v.tensor_tensor(
                            iv, uv, bbr, ALU.mult), r=[ub_, bb], w=[inp[ib]])
                        for j in range(4):
                            et = qd * 4 + j
                            em.emit("v", lambda v, ib=ib, j_=j, e_=et, n_=n, hr3=hr3: v.tensor_tensor_scan(
                                hb[ib].ap[:, j_ * 512:(j_ + 1) * 512],
                                dA[ib].ap[:, j_ * 512:(j_ + 1) * 512],
                                inp[ib].ap[:, j_ * 512:(j_ + 1) * 512],
                                hr3[:, e_:e_ + 1, n_:n_ + 1], ALU.mult, ALU.add),
                                r=[dA[ib], inp[ib], hs_r],
                                w=([hb[ib]] if j == 3 else []), inc=(j == 3))
                        em.emit("v", lambda v, ib=ib, q_=qd, n_=n, hw3=hw3: v.tensor_copy(
                            hw3[:, 4 * q_:4 * q_ + 4, n_:n_ + 1],
                            hb[ib].ap[:, :2048].rearrange(
                                "p (a b) -> p a b", b=512)[:, :, 511:512]),
                            r=[hb[ib]], w=[hs_w])
                        cbr = cb.ap[:, jn * TC:(jn + 1) * TC][:, None, :] \
                            .broadcast_to([128, 4, TC])
                        hv = hb[ib].ap[:, :2048].rearrange("p (a b) -> p a b", b=TC)
                        hcv = hc[ib].ap[:, :2048].rearrange("p (a b) -> p a b", b=TC)
                        em.emit("v", lambda v, hcv=hcv, hv=hv, cbr=cbr, ib=ib: v.tensor_tensor(
                            hcv, hv, cbr, ALU.mult), r=[hb[ib], cb], w=[hc[ib]])
                        for j in range(4):
                            last = (n == NST - 1)
                            em.emit("t", lambda t, ib=ib, j_=j, n_=n: t.matmul(
                                ypsum.ap[:, j_ * 512:(j_ + 1) * 512], self.id16.ap,
                                hc[ib].ap[:, j_ * 512:(j_ + 1) * 512],
                                start=(n_ == 0), stop=(n_ == NST - 1)),
                                r=[self.id16, hc[ib]],
                                w=([ypsum] if (last and j == 3) else []),
                                wfirst=([ypsum] if (n == 0 and j == 0) else []),
                                inc=(j == 3))
                    if ng % 8 == 7 and moe_done[c] < 2 * (qd + 1):
                        ex = moe_done[c]
                        moe_done[c] += 1
                        self._moe_expert(c, ex, heb, gb, auxp)
                zqv = [gb, xpb, mb, hetmp]
                for j in range(4):
                    et = qd * 4 + j
                    em.emit("q", lambda e, i=et, jj=j, cc=c: e.dma_start(
                        self.B[20].ap[:, jj * 512:(jj + 1) * 512],
                        self.d_z.ap[i][:, cc * TC:(cc + 1) * TC]),
                        r=[self.d_z], w=[zqv[j]])
                em.emit("v", lambda v, q_=qd: v.scalar_tensor_tensor(
                    y2[q_].ap[:, :2048], ypsum.ap, 1.0, self.B[20].ap[:, :2048],
                    ALU.mult, ALU.mult), r=[ypsum] + zqv, w=[y2[qd]])
            while moe_done[c] < NE:
                ex = moe_done[c]
                moe_done[c] += 1
                self._moe_expert(c, ex, heb, gb, auxp)
            if c == 0:
                self.dump("y2c0", y2, 2048, BF16)
            wst = self.B[15]
            w3 = wst.ap[:, :2048].rearrange("p (a b) -> p a b", b=128)
            for d in range(KD):
                pp = auxp[d % 2]
                em.emit("g", lambda e, d_=d, w3=w3: e.dma_start(
                    w3, self.d_opw16.ap[:, d_].transpose([1, 0, 2])), r=[self.d_opw16],
                    w=[wst], dma=True)
                for k in range(ET):
                    em.emit("t", lambda t, p_=pp, kk=k, w3=w3: t.matmul(
                        p_.ap, w3[:, kk, :],
                        y2[kk // 4].ap[:, (kk % 4) * 512:(kk % 4 + 1) * 512],
                        start=(kk == 0), stop=(kk == ET - 1)),
                        r=[wst, y2[k // 4]],
                        w=([pp] if k == ET - 1 else []),
                        wfirst=([pp] if k == 0 else []), inc=True)
                em.emit("v", lambda v, p_=pp, d_=d, cc=c: v.scalar_tensor_tensor(
                    xpb.ap, p_.ap, 1.0,
                    self.xn[d_].ap[:, cc * TC:(cc + 1) * TC], ALU.mult, ALU.add),
                    r=[pp, self.xn[d]], w=[xpb])
                em.emit("q", lambda e, d_=d, cc=c: e.dma_start(
                    self.d_xp.ap[d_][:, cc * TC:(cc + 1) * TC], xpb.ap),
                    r=[xpb], w=[self.d_xp])

    def _moe_expert(self, c, ex, heb, gb, auxp):
        em = self.em
        he = heb[ex % 2]
        wst = self.B[15]
        w3 = wst.ap[:, :2048].rearrange("p (a b) -> p a b", b=512)
        g3 = self.grow.ap.rearrange("p (a b) -> p a b", b=TC)
        em.emit("t", lambda t, e_=ex: t.matmul(
            auxp[0].ap, self.ones16.ap, g3[0:1, e_, :],
            start=True, stop=True), r=[self.ones16, self.grow], w=[auxp[0]])
        em.emit("s", lambda s: s.copy(gb.ap, auxp[0].ap), r=[auxp[0]], w=[gb])
        for m in range(4):
            pq = auxp[1]
            for h in range(2):
                em.emit("g", lambda e, e_=ex, h=h, w3=w3: e.dma_start(
                    w3, self.d_w116.ap[e_, 4 * h:4 * h + 4].transpose([1, 0, 2])),
                    r=[self.d_w116], w=[wst], dma=True)
                for kk in range(4):
                    k = 4 * h + kk
                    em.emit("t", lambda t, p_=pq, kk=kk, k=k, mm=m, cc=c, w3=w3: t.matmul(
                        p_.ap, w3[:, kk, mm * 128:(mm + 1) * 128],
                        self.xn[k].ap[:, cc * TC:(cc + 1) * TC],
                        start=(k == 0), stop=(k == KD - 1)),
                        r=[wst, self.xn[k]], w=([pq] if k == KD - 1 else []),
                        wfirst=([pq] if k == 0 else []), inc=True)
            hetmp = self.moe_tmp[1]
            em.emit("s", lambda s, p_=pq, e_=ex, mm=m, h_=hetmp: s.activation(
                h_.ap, p_.ap, AF.Gelu,
                bias=self.b1col.ap[:, e_ * 4 + mm:e_ * 4 + mm + 1]),
                r=[pq, self.b1col], w=[hetmp])
            em.emit("v", lambda v, mm=m, h_=hetmp: v.tensor_tensor(
                he.ap[:, mm * TC:(mm + 1) * TC], h_.ap, gb.ap, ALU.mult),
                r=[hetmp, gb], w=[he])
        for h in range(2):
            em.emit("g", lambda e, e_=ex, h=h, w3=w3: e.dma_start(
                w3, self.d_w216.ap[e_, :, :, h * 512:(h + 1) * 512].transpose([1, 0, 2])),
                r=[self.d_w216], w=[wst], dma=True)
            for dd in range(4):
                d = 4 * h + dd
                pq = auxp[1]
                for k in range(4):
                    em.emit("t", lambda t, p_=pq, k=k, dd=dd, w3=w3: t.matmul(
                        p_.ap, w3[:, k, dd * 128:(dd + 1) * 128],
                        he.ap[:, k * TC:(k + 1) * TC],
                        start=(k == 0), stop=(k == 3)),
                        r=[wst, he], w=([pq] if k == 3 else []),
                        wfirst=([pq] if k == 0 else []), inc=True)
                mb = self.moe_tmp[0]
                em.emit("s", lambda s, p_=pq, m_=mb: s.copy(m_.ap, p_.ap),
                        r=[pq], w=[mb])
                em.emit("g", lambda e, d_=d, cc=c, m_=mb: e.dma_start(
                    self.d_moe.ap[d_][:, cc * TC:(cc + 1) * TC], m_.ap,
                    accum_op=ALU.add), r=[mb], w=[self.d_moe], dma=True)

    def _p4_ffn_final(self):
        for hf in range(2):
            self._p4_half(hf)
            self.em.barrier()
        if P4SUB < 8:
            em = self.em
            for tt in range(TT):
                em.emit("q", lambda e, i=tt: e.dma_start(
                    self.o_out.ap[i * 128:(i + 1) * 128, :],
                    self.F[0].ap[:, :D]), r=[self.F[0]], w=[self.o_out])
            em.barrier()

    def _p4_half(self, hf):
        em = self.em
        TH = T // 2
        t0 = hf * TH
        xp = self.B[0:8]
        pre = self.B[8:16]
        fbs = self.B[16:24]

        def fb_ap(m):
            return fbs[m // 2].ap[:, (m % 2) * TH:(m % 2 + 1) * TH]

        def fb_buf(m):
            return fbs[m // 2]

        gtmp = self.view(self.B[24], self.B[24].ap[:, :TH], "gtmp")
        u2t = self.view(self.B[24], self.B[24].ap[:, TH:2 * TH], "u2t")
        rb = self.view(self.B[25], self.B[25].ap[:, :TH], "rb4")
        sqb = [self.view(self.B[25], self.B[25].ap[:, TH:2 * TH], "sq0")] * 2
        f1b = self.F[1].ap.bitcast(BF16)
        xfb = self.view(self.F[1], f1b[:, :TH], "xfb")
        for k in range(KD):
            em.emit("q", lambda e, i=k, b=xp[k], o=t0: e.dma_start(
                b.ap[:, :TH], self.d_xp.ap[i][:, o:o + TH]),
                r=[self.d_xp], w=[xp[k]])
        prow = self.view(self.PA, self.PA.ap[0:1, :1024], "prow")
        for k in range(KD):
            sq = sqb[k % 2]
            em.emit("s", lambda s, i=k, q=sq: s.activation(
                q.ap, xp[i].ap[:, :TH], AF.Square), r=[xp[k]], w=[sq])
            for f in range(2):
                em.emit("t", lambda t, kk=k, ff=f, q=sq: t.matmul(
                    prow.ap[:, ff * 512:(ff + 1) * 512], self.onecol16.ap,
                    q.ap[:, ff * 512:(ff + 1) * 512],
                    start=(kk == 0), stop=(kk == KD - 1)),
                    r=[self.onecol16, sq],
                    w=([prow, self.PA] if (k == KD - 1 and f == 1) else []),
                    wfirst=([prow, self.PA] if (k == 0 and f == 0) else []),
                    inc=(f == 1))
        sr = self.srow
        em.emit("v", lambda v: v.tensor_scalar(sr.ap[:, :TH], prow.ap[:, :TH],
                1.0 / D, None, ALU.mult), r=[prow], w=[sr])
        em.emit("s", lambda s: s.activation(sr.ap[:, TH:2 * TH], sr.ap[:, :TH],
                AF.Sqrt, bias=self.consts.ap[:1, 0:1]),
                r=[sr, self.consts], w=[sr])
        em.emit("v", lambda v: v.reciprocal(sr.ap[:, 2 * TH:3 * TH],
                sr.ap[:, TH:2 * TH]), r=[sr], w=[sr])
        if P4SUB < 2:
            return
        em.barrier()
        for f in range(2):
            em.emit("t", lambda t, ff=f: t.matmul(
                self.PA.ap[:, ff * 512:(ff + 1) * 512], self.ones32.ap,
                sr.ap[:, 2 * TH + ff * 512:2 * TH + (ff + 1) * 512],
                start=True, stop=True), r=[self.ones32, sr],
                w=([self.PA] if f == 1 else []),
                wfirst=([self.PA] if f == 0 else []), inc=(f == 1))
        em.emit("s", lambda s: s.copy(rb.ap, self.PA.ap[:, :TH]),
                r=[self.PA], w=[rb])
        for d in range(KD):
            t1 = sqb[0]
            em.emit("q", lambda e, d_=d, b=xfb, o=t0: e.dma_start(
                b.ap, self.d_xf.ap[d_][:, o:o + TH]), r=[self.d_xf], w=[xfb])
            em.emit("v", lambda v, d_=d, b=xfb, t1=t1: v.tensor_tensor(
                t1.ap, b.ap, xp[d_].ap[:, :TH], ALU.add),
                r=[xfb, xp[d]], w=[t1])
            em.emit("q", lambda e, d_=d, b=xfb, o=t0: e.dma_start(
                b.ap, self.d_moe.ap[d_][:, o:o + TH]),
                r=[self.d_moe], w=[xfb])
            em.emit("v", lambda v, d_=d, b=xfb, t1=t1: v.tensor_tensor(
                pre[d_].ap[:, :TH], t1.ap, b.ap, ALU.add),
                r=[xfb, t1], w=[pre[d]])
        if P4SUB < 7:
            return
        gust = self.F[3]
        gub = gust.ap.bitcast(BF16).rearrange("p (a b) -> p a b", b=128)
        for m in range(ET):
            pg, pu = self.PA, self.PB
            em.emit("g", lambda e, mm=m, gub=gub: e.dma_start(
                gub[:, 0:8, :], self.d_gud16.ap[0, :, :, mm * 128:(mm + 1) * 128].transpose([1, 0, 2])),
                r=[self.d_gud16], w=[gust], dma=True)
            em.emit("g", lambda e, mm=m, gub=gub: e.dma_start(
                gub[:, 8:16, :], self.d_gud16.ap[1, :, :, mm * 128:(mm + 1) * 128].transpose([1, 0, 2])),
                r=[self.d_gud16], w=[gust], dma=True)
            for k in range(KD):
                for f in range(2):
                    em.emit("t", lambda t, pp=pg, kk=k, ff=f, gub=gub: t.matmul(
                        pp.ap[:, ff * 512:(ff + 1) * 512], gub[:, kk, :],
                        xp[kk].ap[:, ff * 512:(ff + 1) * 512],
                        start=(kk == 0), stop=(kk == KD - 1)),
                        r=[gust, xp[k]],
                        w=([pg] if (k == KD - 1 and f == 1) else []),
                        wfirst=([pg] if (k == 0 and f == 0) else []),
                        inc=(f == 1))
                    em.emit("t", lambda t, pp=pu, kk=k, ff=f, gub=gub: t.matmul(
                        pp.ap[:, ff * 512:(ff + 1) * 512], gub[:, 8 + kk, :],
                        xp[kk].ap[:, ff * 512:(ff + 1) * 512],
                        start=(kk == 0), stop=(kk == KD - 1)),
                        r=[gust, xp[k]],
                        w=([pu] if (k == KD - 1 and f == 1) else []),
                        wfirst=([pu] if (k == 0 and f == 0) else []),
                        inc=(f == 1))
            em.emit("v", lambda v: v.tensor_tensor(gtmp.ap,
                    self.PA.ap[:, :TH], rb.ap, ALU.mult),
                    r=[self.PA, rb], w=[gtmp])
            em.emit("s", lambda s: s.activation(xfb.ap, gtmp.ap, AF.Silu),
                    r=[gtmp], w=[xfb])
            em.emit("v", lambda v: v.tensor_tensor(u2t.ap, self.PB.ap[:, :TH],
                    rb.ap, ALU.mult), r=[self.PB, rb], w=[u2t])
            em.emit("v", lambda v, i=m: v.tensor_tensor(fb_ap(i), xfb.ap,
                    u2t.ap, ALU.mult), r=[xfb, u2t], w=[fb_buf(m)])
        if P4SUB < 8:
            return
        otm = [self.view(self.F[2], self.F[2].ap[:, :512], "otm0"),
               self.view(self.F[2], self.F[2].ap[:, 512:1024], "otm1")]
        ofm = self.F[0]
        dwst = self.F[4]
        dwb = dwst.ap.bitcast(BF16).rearrange("p (a b) -> p a b", b=128)
        for d in range(KD):
            pp = self.PA if d % 2 == 0 else self.PB
            em.emit("g", lambda e, d_=d, dwb=dwb: e.dma_start(
                dwb[:, 0:16, :], self.d_dw16.ap[:, :, d_ * 128:(d_ + 1) * 128].transpose([1, 0, 2])),
                r=[self.d_dw16], w=[dwst], dma=True)
            for k in range(ET):
                for f in range(2):
                    em.emit("t", lambda t, pp=pp, kk=k, ff=f, dwb=dwb: t.matmul(
                        pp.ap[:, ff * 512:(ff + 1) * 512], dwb[:, kk, :],
                        fb_ap(kk)[:, ff * 512:(ff + 1) * 512],
                        start=(kk == 0), stop=(kk == ET - 1)),
                        r=[dwst, fb_buf(k)],
                        w=([pp] if (k == ET - 1 and f == 1) else []),
                        wfirst=([pp] if (k == 0 and f == 0) else []),
                        inc=(f == 1))
            em.emit("v", lambda v, pp=pp, d_=d: v.scalar_tensor_tensor(
                ofm.ap[:, :TH], pp.ap[:, :TH], 1.0, pre[d_].ap[:, :TH],
                ALU.mult, ALU.add), r=[pp, pre[d]], w=[ofm])
            po = self.PB if d % 2 == 0 else self.PA
            for tq in range(2):
                o = otm[tq % 2]
                for j in range(4):
                    tt = tq * 4 + j
                    em.emit("t", lambda t, pp=po, jj=j, i=tt: t.transpose(
                        pp.ap[:, 1024 + jj * 128:1024 + (jj + 1) * 128],
                        ofm.ap[:, i * 128:(i + 1) * 128], self.id32.ap),
                        r=[ofm, self.id32], w=([po] if j == 3 else []),
                        wfirst=([po] if j == 0 else []), inc=(j == 3))
                em.emit("v", lambda v, pp=po, o=o: v.tensor_copy(
                    o.ap, pp.ap[:, 1024:1536]), r=[po], w=[o])
                for j in range(4):
                    tt = tq * 4 + j
                    em.emit("q", lambda e, o=o, d_=d, i=tt, jj=j, o0=t0: e.dma_start(
                        self.o_out.ap[o0 + i * 128:o0 + (i + 1) * 128,
                                      d_ * 128:(d_ + 1) * 128],
                        o.ap[:, jj * 128:(jj + 1) * 128]),
                        r=[o], w=[self.o_out])


def build_program(dbg=()):
    kb = KB(dbg=dbg)
    nc = kb.build()
    return nc


def _in_maps(inputs):
    f32 = np.float32
    x = np.ascontiguousarray(np.asarray(inputs["L0_attn_x"], dtype=f32))

    def g(k):
        return np.ascontiguousarray(np.asarray(inputs[k], dtype=f32))

    common = {
        "inw": g("L0_attn_in_proj_W"),
        "convw": g("L0_attn_conv_W").reshape(4, E),
        "convb": g("L0_attn_conv_b"),
        "dtw": g("L0_attn_dt_W"),
        "dtb": g("L0_attn_dt_b"),
        "bcw": np.ascontiguousarray(np.concatenate(
            [g("L0_attn_B_W"), g("L0_attn_C_W")], axis=1)),
        "alog": g("L0_attn_A_log").reshape(1, NST),
        "opw": g("L0_attn_out_proj_W"),
        "gw": g("L0_attn_gate_W"),
        "uw": g("L0_attn_up_W"),
        "dw": g("L0_attn_down_W"),
        "normw": g("L0_attn_input_norm_w"),
        "fnormw": g("L0_attn_ffn_norm_w"),
        "w1": g("L0_ffn_w1"),
        "b1": g("L0_ffn_b1"),
        "w2": g("L0_ffn_w2"),
        "b2": g("L0_ffn_b2"),
        "cent": g("L0_ffn_centroids"),
        "wroute": g("L0_ffn_w_route"),
    }
    return [dict(common, x=np.ascontiguousarray(x[i])) for i in range(N_CORES)]


def kernel(**inputs):
    nc = build_program(dbg=DBG)
    res = run_bass_kernel_spmd(nc, _in_maps(inputs), core_ids=list(range(N_CORES)))
    out = np.stack([res.results[i]["out"] for i in range(N_CORES)], axis=0)
    return np.asarray(out, dtype=np.float32)


# version 05
# speedup vs baseline: 1.0679x; 1.0679x over previous
import numpy as np
import ml_dtypes
from contextlib import ExitStack

import concourse.bass as bass
import concourse.mybir as mybir
from concourse.bass_utils import run_bass_kernel_spmd

F32 = mybir.dt.float32
BF16 = mybir.dt.bfloat16
AF = mybir.ActivationFunctionType
ALU = mybir.AluOpType
AX = mybir.AxisListType.X

T = 2048
D = 1024
E = 2048
NST = 64
ET = 16
KD = 8
NCH = 4
TC = 512
NE = 8
DE = 512
TT = 16
NG = 4

DBG = []
PHASES = 5
P4SUB = 9
N_CORES = 8


class Buf:
    __slots__ = ("ap", "w", "r", "name")

    def __init__(self, ap, name=""):
        self.ap = ap
        self.w = None
        self.r = {}
        self.name = name


class EM:
    ENGS = ("v", "s", "t", "g", "q")

    def __init__(self, nc, ctx):
        self.nc = nc
        self.plan = {e: [] for e in self.ENGS}
        self.sems = {k: ctx.enter_context(nc.semaphore(f"em_{k}"))
                     for k in ("v", "s", "t", "g", "qd", "gd", "qp", "gp")}
        self.count = {k: 0 for k in self.sems}
        self.mark = {e: {} for e in self.ENGS}
        self.pub_covered = {"qd": 0, "gd": 0}

    def _wait(self, eng, key, cnt):
        if cnt <= 0:
            return
        if key in ("qd", "gd"):
            issuer = "q" if key == "qd" else "g"
            pubkey = "qp" if key == "qd" else "gp"
            if eng == issuer:
                cnt = self.count[key]
            else:
                if self.pub_covered[key] < cnt:
                    total = self.count[key]
                    self._wait(issuer, key, total)
                    self.count[pubkey] += 1
                    sem = self.sems[pubkey]
                    self.plan[issuer].append(
                        lambda e, sem=sem: e.nop().then_inc(sem, 1))
                    self.pub_covered[key] = total
                self._wait(eng, pubkey, self.count[pubkey])
                return
        if self.mark[eng].get(key, 0) >= cnt:
            return
        self.mark[eng][key] = cnt
        sem = self.sems[key]
        val = cnt * 16 if key in ("qd", "gd") else cnt
        self.plan[eng].append(lambda e, sem=sem, val=val: e.wait_ge(sem, val))

    def emit(self, eng, fn, r=(), w=(), inc=True, dma=False, wfirst=()):
        if eng == "q":
            semkey, dma = "qd", True
        elif eng == "g" and dma:
            semkey = "gd"
        else:
            semkey = eng
        for b in r:
            if b.w is not None:
                self._wait(eng, *b.w)
        for b in list(w) + list(wfirst):
            if b.w is not None:
                self._wait(eng, *b.w)
            for key, cnt in b.r.items():
                self._wait(eng, key, cnt)
        if inc:
            self.count[semkey] += 1
            cnt = self.count[semkey]
            sem = self.sems[semkey]
            amt = 16 if semkey in ("qd", "gd") else 1
            self.plan[eng].append(
                lambda e, fn=fn, sem=sem, amt=amt: fn(e).then_inc(sem, amt))
        else:
            cnt = self.count[semkey] + 1
            self.plan[eng].append(lambda e, fn=fn: fn(e))
        for b in r:
            if b.r.get(semkey, 0) < cnt:
                b.r[semkey] = cnt
        for b in w:
            b.w = (semkey, cnt)
            b.r = {}
        return cnt

    def barrier(self):
        for eng in self.ENGS:
            for key in self.sems:
                self._wait(eng, key, self.count[key])

    def replay(self, block):
        block.vector(lambda e: [f(e) for f in self.plan["v"]])
        block.scalar(lambda e: [f(e) for f in self.plan["s"]])
        block.tensor(lambda e: [f(e) for f in self.plan["t"]])
        block.gpsimd(lambda e: [f(e) for f in self.plan["g"]])
        block.sync(lambda e: [f(e) for f in self.plan["q"]])


class KB:
    def __init__(self, dbg=()):
        self.nc = bass.Bass("TRN2", target_bir_lowering=False)
        self.ctx = ExitStack()
        self.em = EM(self.nc, self.ctx)
        self.dbg_names = set(dbg)
        self._uid = 0

    def sb(self, name, shape, dtype):
        h = self.ctx.enter_context(self.nc.sbuf_tensor(name, shape, dtype))
        return Buf(h.ap(), name)

    def dram(self, name, shape, dtype, kind="Internal"):
        return Buf(self.nc.dram_tensor(name, shape, dtype, kind=kind).ap(), name)

    def view(self, buf, ap, name=""):
        return Buf(ap, name or buf.name + "/v")

    def dump(self, name, bufs, width=T, dtype=F32):
        if name not in self.dbg_names:
            return
        d = self.dram("dbg_" + name, [len(bufs), 128, width], dtype,
                      kind="ExternalOutput")
        for i, b in enumerate(bufs):
            self.em.emit("q", lambda e, i=i, b=b: e.dma_start(
                d.ap[i], b.ap[:, :width]), r=[b], w=[d])

    def build(self):
        em = self.em
        self.i_x = self.dram("x", [T, D], F32, kind="ExternalInput")
        self.i_inw = self.dram("inw", [D, 2 * E], F32, kind="ExternalInput")
        self.i_convw = self.dram("convw", [4, E], F32, kind="ExternalInput")
        self.i_convb = self.dram("convb", [E], F32, kind="ExternalInput")
        self.i_dtw = self.dram("dtw", [E, E], F32, kind="ExternalInput")
        self.i_dtb = self.dram("dtb", [E], F32, kind="ExternalInput")
        self.i_bcw = self.dram("bcw", [E, 2 * NST], F32, kind="ExternalInput")
        self.i_alog = self.dram("alog", [1, NST], F32, kind="ExternalInput")
        self.i_opw = self.dram("opw", [E, D], F32, kind="ExternalInput")
        self.i_gw = self.dram("gw", [D, E], F32, kind="ExternalInput")
        self.i_uw = self.dram("uw", [D, E], F32, kind="ExternalInput")
        self.i_dw = self.dram("dw", [E, D], F32, kind="ExternalInput")
        self.i_normw = self.dram("normw", [D], F32, kind="ExternalInput")
        self.i_fnormw = self.dram("fnormw", [D], F32, kind="ExternalInput")
        self.i_w1 = self.dram("w1", [NE, D, DE], F32, kind="ExternalInput")
        self.i_b1 = self.dram("b1", [NE, DE], F32, kind="ExternalInput")
        self.i_w2 = self.dram("w2", [NE, DE, D], F32, kind="ExternalInput")
        self.i_b2 = self.dram("b2", [NE, D], F32, kind="ExternalInput")
        self.i_cent = self.dram("cent", [NE, D], F32, kind="ExternalInput")
        self.i_wroute = self.dram("wroute", [NE, D], F32, kind="ExternalInput")
        self.o_out = self.dram("out", [T, D], F32, kind="ExternalOutput")

        self.d_dt = self.dram("d_dt", [ET, 128, T], F32)
        self.d_u = self.dram("d_u", [ET, 128, T], BF16)
        self.d_z = self.dram("d_z", [ET, 128, T], BF16)
        self.d_xf = self.dram("d_xf", [KD, 128, T], BF16)
        self.d_xp = self.dram("d_xp", [KD, 128, T], BF16)
        self.d_moe = self.dram("d_moe", [KD, 128, T], BF16)
        self.d_xpv = [self.view(self.d_xp, self.d_xp.ap[i], f"d_xp{i}")
                      for i in range(KD)]
        self.d_moev = [self.view(self.d_moe, self.d_moe.ap[i], f"d_moe{i}")
                       for i in range(KD)]
        self.d_opw16 = self.dram("d_opw16", [ET, KD, 128, 128], BF16)
        self.d_w116 = self.dram("d_w116", [NE, KD, 128, DE], BF16)
        self.d_w216 = self.dram("d_w216", [NE, 4, 128, D], BF16)
        self.d_gud16 = self.dram("d_gud16", [2, KD, 128, E], BF16)
        self.d_dw16 = self.dram("d_dw16", [ET, 128, D], BF16)
        self.d_bc = self.dram("d_bc", [128, T], BF16)
        self.d_g = self.dram("d_g", [NE, T], BF16)

        self.F = [self.sb(f"fa{i}", [128, 2048], F32) for i in range(5)]
        self.B = [self.sb(f"ba{i}", [128, 2052], BF16) for i in range(26)]
        self.xn = self.B[0:8]

        self.ones16 = self.sb("ones16", [1, 128], BF16)
        self.ones32 = self.sb("ones32", [1, 128], F32)
        self.onecol16 = self.sb("onecol16", [128, 1], BF16)
        self.id16 = self.sb("id16", [128, 128], BF16)
        self.id32 = self.sb("id32", [128, 128], F32)
        self.consts = self.sb("consts", [128, 8], F32)
        self.scol = self.sb("scol", [128, 512], F32)
        self.wcol = self.sb("wcol", [128, KD], F32)
        self.fwcol = self.sb("fwcol", [128, KD], F32)
        self.cbcol = self.sb("cbcol", [128, ET], F32)
        self.dbcol = self.sb("dbcol", [128, ET], F32)
        self.cwcol = self.sb("cwcol", [128, ET * 4], F32)
        self.b1col = self.sb("b1col", [128, NE * 4], F32)
        self.asb = self.sb("asb", [128, NST], F32)
        self.arow = self.sb("arow", [1, NST], F32)
        self.bcsb = self.sb("bcsb", [128, T], BF16)
        self.gatesf = self.sb("gatesf", [NE, T], BF16)
        self.b2sb = self.sb("b2sb", [NE, D], BF16)
        self.hstate = [self.sb(f"hst{i}", [128, ET * NST], F32) for i in range(2)]
        self.rowT = self.sb("rowT", [TT, 128], F32)
        self.brow = [self.sb(f"brow{i}", [1, 2 * NG * TC], BF16) for i in range(2)]
        self.grow = self.sb("grow", [1, NE * TC], BF16)
        self.srow = self.sb("srow", [1, 3 * 1024], F32)
        self.rowflat = Buf(self.srow.ap[:, :2048], "rowflat")
        self.wf32 = [self.sb(f"wf{i}", [128, 128], F32) for i in range(2)]
        self.wb16 = [self.sb(f"wb{i}", [128, 128], BF16) for i in range(8)]

        PA = self.ctx.enter_context(self.nc.psum_tensor("PA", [128, 2048], F32))
        PB = self.ctx.enter_context(self.nc.psum_tensor("PB", [128, 2048], F32))
        self.PA = Buf(PA.ap(), "PA")
        self.PB = Buf(PB.ap(), "PB")

        self._p0_consts()
        if PHASES >= 1:
            self._p1_load_ln()
            em.barrier()
        if PHASES >= 2:
            self._p2_mixer_pre()
            em.barrier()
        if PHASES >= 3:
            self._p25_routing()
            em.barrier()
        if PHASES >= 4:
            self._p3_scan()
            em.barrier()
        if PHASES >= 5:
            self._p4_ffn_final()
            em.barrier()
        else:
            for tt in range(TT):
                em.emit("q", lambda e, i=tt: e.dma_start(
                    self.o_out.ap[i * 128:(i + 1) * 128, :],
                    self.F[0].ap[:, :D]), r=[self.F[0]], w=[self.o_out])
            em.barrier()

        for nm in list(self.dbg_names):
            if not nm.startswith("d_"):
                continue
            t = getattr(self, nm, None)
            if t is None:
                continue
            sh = list(t.ap.shape)
            o = self.dram("dbgo_" + nm, sh, t.ap.dtype, kind="ExternalOutput")
            em.emit("q", lambda e, o=o, t=t: e.dma_start(o.ap, t.ap),
                    r=[t], w=[o])
        em.barrier()

        block = self.ctx.enter_context(self.nc.Block())
        em.replay(block)
        self.ctx.close()
        return self.nc

    def _p0_consts(self):
        em = self.em
        for b, val in ((self.ones16, 1.0), (self.ones32, 1.0), (self.onecol16, 1.0)):
            em.emit("g", lambda g, b=b, v=val: g.memset(b.ap, v), w=[b])
        for ident in (self.id16, self.id32):
            em.emit("g", lambda g, a=ident.ap: g.memset(a, 0.0), w=[ident])
            em.emit("g", lambda g, a=ident.ap: g.affine_select(
                out=a, in_=a, compare_op=ALU.not_equal, fill=1.0,
                base=0, pattern=[[-1, 128]], channel_multiplier=1),
                r=[ident], w=[ident])
        for col, val in ((0, 1e-5), (1, 1.0 / D), (2, -0.5), (3, 1.5), (4, 1e-12)):
            em.emit("g", lambda g, c=col, v=val: g.memset(
                self.consts.ap[:, c:c + 1], v), w=[self.consts])

        def vload(dst, src_ap, ncols):
            v2 = src_ap.rearrange("(a b) -> a b", b=128).transpose([1, 0])
            em.emit("q", lambda e, d=dst, s=v2, n=ncols: e.dma_start(
                d.ap[:, :n], s, allow_slow_non_contiguous=True), w=[dst])

        vload(self.wcol, self.i_normw.ap, KD)
        vload(self.fwcol, self.i_fnormw.ap, KD)
        vload(self.cbcol, self.i_convb.ap, ET)
        vload(self.dbcol, self.i_dtb.ap, ET)
        for wd in range(4):
            cv = self.i_convw.ap[wd].rearrange("(i b) -> b i", b=128)
            cwd = self.cwcol.ap.rearrange("p (i w) -> p i w", w=4)[:, :, wd]
            em.emit("q", lambda e, cv=cv, cwd=cwd: e.dma_start(cwd, cv,
                    allow_slow_non_contiguous=True), w=[self.cwcol])
        for e_ in range(NE):
            b1v = self.i_b1.ap[e_].rearrange("(m b) -> b m", b=128)
            b1d = self.b1col.ap[:, e_ * 4:(e_ + 1) * 4]
            em.emit("q", lambda e, b1v=b1v, b1d=b1d: e.dma_start(b1d, b1v,
                    allow_slow_non_contiguous=True), w=[self.b1col])
        em.emit("q", lambda e: e.dma_start(self.arow.ap, self.i_alog.ap), w=[self.arow])
        em.emit("s", lambda s: s.activation(self.arow.ap, self.arow.ap, AF.Exp),
                r=[self.arow], w=[self.arow])
        em.emit("v", lambda v: v.tensor_scalar(self.arow.ap, self.arow.ap, -1.0,
                None, ALU.mult), r=[self.arow], w=[self.arow])
        em.emit("t", lambda t: t.matmul(self.PA.ap[:, :NST], self.ones32.ap,
                self.arow.ap, start=True, stop=True),
                r=[self.ones32, self.arow], w=[self.PA])
        em.emit("s", lambda s: s.copy(self.asb.ap, self.PA.ap[:, :NST]),
                r=[self.PA], w=[self.asb])
        b2f = self.F[4]
        em.emit("q", lambda e: e.dma_start(b2f.ap[:NE, :D], self.i_b2.ap), w=[b2f])
        em.emit("v", lambda v: v.tensor_copy(self.b2sb.ap, b2f.ap[:NE, :D]),
                r=[b2f], w=[self.b2sb])

    def _rsqrt(self, src_ap, dst_ap, rbufs, wbufs, scr_ap, scrbuf, eps_col=0):
        em = self.em
        n = src_ap.shape[-1]
        em.emit("s", lambda s: s.activation(scr_ap[:, :n], src_ap, AF.Sqrt,
                bias=self.consts.ap[:src_ap.shape[0], eps_col:eps_col + 1]),
                r=rbufs + [self.consts], w=[scrbuf])
        em.emit("v", lambda v: v.reciprocal(dst_ap, scr_ap[:, :n]),
                r=[scrbuf], w=wbufs)

    def _wtile(self, src_ap, idx, scale_col=None):
        em = self.em
        f = self.wf32[idx % 2]
        b = self.wb16[idx % 8]
        em.emit("g", lambda e, s=src_ap, d=f.ap: e.dma_start(d, s), w=[f], dma=True)
        if scale_col is None:
            em.emit("v", lambda v, s=f.ap, d=b.ap: v.tensor_copy(d, s), r=[f], w=[b])
        else:
            col, cbuf = scale_col
            em.emit("v", lambda v, s=f.ap, d=b.ap, c=col: v.tensor_scalar(
                d, s, c, None, ALU.mult), r=[f, cbuf], w=[b])
        return b

    def _bcast_row(self, src_row_ap, rbufs, dst_ap, wbuf, psum, width, f32=False):
        em = self.em
        ones = self.ones32 if f32 else self.ones16
        em.emit("t", lambda t: t.matmul(psum.ap[:, :width], ones.ap, src_row_ap,
                start=True, stop=True), r=[ones] + rbufs, w=[psum])
        em.emit("s", lambda s: s.copy(dst_ap, psum.ap[:, :width]),
                r=[psum], w=[wbuf])

    def _p1_load_ln(self):
        em = self.em
        sums = self.view(self.scol, self.scol.ap[:, 0:32], "lnsums")
        mcol = self.view(self.scol, self.scol.ap[:, 32:48], "mcol")
        varc = self.view(self.scol, self.scol.ap[:, 48:64], "varc")
        rstd = self.view(self.scol, self.scol.ap[:, 64:80], "rstd")
        nsc = self.view(self.scol, self.scol.ap[:, 80:112], "nsc")
        rmss = self.view(self.scol, self.scol.ap[:, 112:128], "rmss")
        self.rms_sumsq = rmss
        self.ln_cols = (mcol, rstd)
        scrt = self.F[4]
        xf = self.B[8:16]
        for tt in range(TT):
            sl = self.F[tt % 4]
            em.emit("q", lambda e, d=sl, i=tt: e.dma_start(
                d.ap[:, :D], self.i_x.ap[i * 128:(i + 1) * 128, :]), w=[sl])
            em.emit("s", lambda s, x=sl, i=tt: s.activation(
                scrt.ap[:, :D], x.ap[:, :D], AF.Copy,
                accum_out=sums.ap[:, i:i + 1]), r=[sl], w=[scrt, sums])
            em.emit("s", lambda s, x=sl, i=tt: s.activation(
                scrt.ap[:, :D], x.ap[:, :D], AF.Square,
                accum_out=sums.ap[:, 16 + i:17 + i]), r=[sl], w=[scrt, sums])
            em.emit("v", lambda v, i=tt: v.tensor_scalar(
                mcol.ap[:, i:i + 1], sums.ap[:, i:i + 1], 1.0 / D, None,
                ALU.mult), r=[sums], w=[mcol])
            em.emit("v", lambda v, i=tt: v.tensor_tensor(
                varc.ap[:, i:i + 1], mcol.ap[:, i:i + 1], mcol.ap[:, i:i + 1],
                ALU.mult), r=[mcol], w=[varc])
            em.emit("v", lambda v, i=tt: v.scalar_tensor_tensor(
                varc.ap[:, i:i + 1], sums.ap[:, 16 + i:17 + i], 1.0 / D,
                varc.ap[:, i:i + 1], ALU.mult, ALU.subtract),
                r=[sums, varc], w=[varc])
            self._rsqrt(varc.ap[:, tt:tt + 1], rstd.ap[:, tt:tt + 1],
                        [varc], [rstd], nsc.ap[:, 0:2], nsc)
            self._transpose_tile(sl, tt, xf, F32)
            em.emit("v", lambda v, x=sl, i=tt: v.tensor_scalar(
                x.ap[:, :D], x.ap[:, :D], mcol.ap[:, i:i + 1],
                rstd.ap[:, i:i + 1], ALU.subtract, ALU.mult),
                r=[sl, mcol, rstd], w=[sl])
            em.emit("s", lambda s, x=sl, i=tt: s.activation(
                scrt.ap[:, :D], x.ap[:, :D], AF.Square,
                accum_out=rmss.ap[:, i:i + 1]), r=[sl], w=[scrt, rmss])
            self._transpose_tile(sl, tt, self.xn, F32)
        for k in range(KD):
            em.emit("q", lambda e, i=k, b=xf[k]: e.dma_start(
                self.d_xf.ap[i], b.ap[:, :T]), r=[xf[k]], w=[self.d_xf])
        self.dump("xn", self.xn, T, BF16)

    def _transpose_tile(self, tile, tt, fm_tiles, in_dtype):
        em = self.em
        ident = self.id32 if in_dtype == F32 else self.id16
        for kq in range(KD // 4):
            p = self.PA if kq % 2 == 0 else self.PB
            for j in range(4):
                k = kq * 4 + j
                em.emit("t", lambda t, x=tile, kk=k, pp=p, jj=j: t.transpose(
                    pp.ap[:, jj * 128:(jj + 1) * 128],
                    x.ap[:, kk * 128:(kk + 1) * 128], ident.ap),
                    r=[tile, ident], w=([p] if j == 3 else []),
                    wfirst=([p] if j == 0 else []), inc=(j == 3))
            for j in range(4):
                k = kq * 4 + j
                em.emit("s", lambda s, kk=k, pp=p, jj=j, i=tt: s.copy(
                    fm_tiles[kk].ap[:, i * 128:(i + 1) * 128],
                    pp.ap[:, jj * 128:(jj + 1) * 128]), r=[p], w=[fm_tiles[k]])

    def _p2_mixer_pre(self):
        em = self.em
        rmsr = self.view(self.scol, self.scol.ap[:, 128:144], "rmsr")
        nsc = self.view(self.scol, self.scol.ap[:, 144:176], "nsc2")
        mean = self.view(self.scol, self.scol.ap[:, 176:192], "rmsmean")
        em.emit("v", lambda v: v.tensor_scalar(mean.ap, self.rms_sumsq.ap,
                1.0 / D, None, ALU.mult), r=[self.rms_sumsq], w=[mean])
        self._rsqrt(mean.ap, rmsr.ap, [mean], [rmsr], nsc.ap, nsc)
        rb = self.F[4]
        self._bcast_cols(rmsr, rb)
        self.dump("rmsr", [rb], T, BF16)

        xz = self.B[8:10]
        xc = self.B[10:26]
        zbuf = self.B[9]
        wi = 0
        ztmp = self.F[2]
        for m in range(32):
            p = self.PA if m % 2 == 0 else self.PB
            order_m = m + 16 if m < 16 else m - 16
            for k in range(KD):
                wb = self._wtile(self.i_inw.ap[k * 128:(k + 1) * 128,
                                               order_m * 128:(order_m + 1) * 128],
                                 wi, scale_col=(self.wcol.ap[:, k:k + 1], self.wcol))
                wi += 1
                for f in range(4):
                    em.emit("t", lambda t, pp=p, w=wb, kk=k, ff=f: t.matmul(
                        pp.ap[:, ff * 512:(ff + 1) * 512], w.ap,
                        self.xn[kk].ap[:, ff * 512:(ff + 1) * 512],
                        start=(kk == 0), stop=(kk == KD - 1)),
                        r=[wb, self.xn[k]],
                        w=([p] if (k == KD - 1 and f == 3) else []),
                        wfirst=([p] if (k == 0 and f == 0) else []),
                        inc=True)
            if order_m >= ET:
                i = order_m - ET
                em.emit("v", lambda v, pp=p: v.tensor_tensor(
                    ztmp.ap, pp.ap, rb.ap[:, :T], ALU.mult), r=[p, rb], w=[ztmp])
                em.emit("s", lambda s: s.activation(zbuf.ap[:, :T], ztmp.ap,
                        AF.Silu), r=[ztmp], w=[zbuf])
                em.emit("q", lambda e, i=i: e.dma_start(self.d_z.ap[i],
                        zbuf.ap[:, :T]), r=[zbuf], w=[self.d_z])
            else:
                i = order_m
                sl = xz[i % 2]
                em.emit("g", lambda g, b=sl: g.memset(b.ap[:, 0:1], 0.0), w=[sl])
                em.emit("g", lambda g, b=sl: g.memset(b.ap[:, T + 1:T + 3], 0.0),
                        r=[sl], w=[sl])
                em.emit("v", lambda v, pp=p, b=sl: v.tensor_tensor(
                    b.ap[:, 1:T + 1], pp.ap, rb.ap[:, :T], ALU.mult),
                    r=[p, rb], w=[sl])
                acc = [self.F[0], self.F[1]]
                em.emit("v", lambda v, i=i, a=acc[0], b=sl: v.tensor_scalar(
                    a.ap, b.ap[:, 0:T], self.cwcol.ap[:, i * 4:i * 4 + 1], None,
                    ALU.mult), r=[sl, self.cwcol], w=[acc[0]])
                for wd in range(1, 4):
                    src_, dst_ = acc[(wd + 1) % 2], acc[wd % 2]
                    em.emit("v", lambda v, i=i, s_=src_, d_=dst_, b=sl, wd=wd:
                            v.scalar_tensor_tensor(
                        d_.ap, b.ap[:, wd:wd + T],
                        self.cwcol.ap[:, i * 4 + wd:i * 4 + wd + 1], s_.ap,
                        ALU.mult, ALU.add), r=[sl, self.cwcol, src_], w=[dst_])
                em.emit("s", lambda s, i=i, a=acc[1]: s.activation(
                    xc[i].ap[:, :T], a.ap, AF.Silu,
                    bias=self.cbcol.ap[:, i:i + 1]), r=[acc[1], self.cbcol], w=[xc[i]])
        self.dump("xc", xc, T, BF16)
        self.dump("xzlast", xz, 2051, BF16)
        self.dump("cwcol", [self.cwcol], 64, F32)
        self.dump("cbcol", [self.cbcol], ET, F32)

        sp1 = self.F[2]
        dtf = [self.F[3], self.F[3]]
        for m in range(ET):
            p = self.PA if m % 2 == 0 else self.PB
            for k in range(ET):
                wb = self._wtile(self.i_dtw.ap[k * 128:(k + 1) * 128,
                                               m * 128:(m + 1) * 128], wi)
                wi += 1
                for f in range(4):
                    em.emit("t", lambda t, pp=p, w=wb, kk=k, ff=f: t.matmul(
                        pp.ap[:, ff * 512:(ff + 1) * 512], w.ap,
                        xc[kk].ap[:, ff * 512:(ff + 1) * 512],
                        start=(kk == 0), stop=(kk == ET - 1)),
                        r=[wb, xc[k]],
                        w=([p] if (k == ET - 1 and f == 3) else []),
                        wfirst=([p] if (k == 0 and f == 0) else []),
                        inc=True)
            d = dtf[m % 2]
            sp2 = self.F[4]
            em.emit("s", lambda s, pp=p, i=m: s.activation(
                sp1.ap, pp.ap, AF.Exp, bias=self.dbcol.ap[:, i:i + 1]),
                r=[p, self.dbcol], w=[sp1])
            em.emit("s", lambda s, sp2=sp2: s.activation(sp2.ap, sp1.ap, AF.Ln,
                    bias=1.0), r=[sp1], w=[sp2])
            em.emit("v", lambda v, d=d, sp2=sp2: v.tensor_scalar(
                d.ap, sp2.ap, 0.001, 0.1, ALU.max, ALU.min), r=[sp2], w=[d])
            em.emit("q", lambda e, d=d, i=m: e.dma_start(self.d_dt.ap[i], d.ap),
                    r=[d], w=[self.d_dt])
            ub = zbuf
            em.emit("v", lambda v, d=d, u=ub, i=m: v.tensor_tensor(
                u.ap[:, :T], d.ap, xc[i].ap[:, :T], ALU.mult),
                r=[d, xc[m]], w=[ub])
            em.emit("q", lambda e, u=ub, i=m: e.dma_start(self.d_u.ap[i],
                    u.ap[:, :T]), r=[ub], w=[self.d_u])
        if "sp" in self.dbg_names:
            pm15 = self.PB
            em.emit("v", lambda v: v.tensor_copy(self.F[1].ap, pm15.ap),
                    r=[pm15], w=[self.F[1]])
        self.dump("dt", dtf, T, F32)
        self.dump("dbcol", [self.dbcol], ET, F32)
        self.dump("sp", [self.F[2], self.F[4], self.F[1]], T, F32)

        p = self.PA
        for k in range(ET):
            wb = self._wtile(self.i_bcw.ap[k * 128:(k + 1) * 128, :], wi)
            wi += 1
            for f in range(4):
                em.emit("t", lambda t, pp=p, w=wb, kk=k, ff=f: t.matmul(
                    pp.ap[:, ff * 512:(ff + 1) * 512], w.ap,
                    xc[kk].ap[:, ff * 512:(ff + 1) * 512],
                    start=(kk == 0), stop=(kk == ET - 1)),
                    r=[wb, xc[k]],
                    w=([p] if (k == ET - 1 and f == 3) else []),
                    wfirst=([p] if (k == 0 and f == 0) else []),
                    inc=True)
        em.emit("s", lambda s: s.copy(self.bcsb.ap, p.ap), r=[p], w=[self.bcsb])
        em.emit("q", lambda e: e.dma_start(self.d_bc.ap, self.bcsb.ap),
                r=[self.bcsb], w=[self.d_bc])
        self.dump("bc", [self.bcsb], T, BF16)

        em.barrier()
        self._precast_weights()

    def _bcast_cols(self, colbuf, out, half=False):
        em = self.em
        p = self.PA
        em.emit("t", lambda t: t.transpose(p.ap[:TT, :128], colbuf.ap,
                self.id32.ap), r=[colbuf, self.id32], w=[p])
        em.emit("s", lambda s: s.copy(self.rowT.ap, p.ap[:TT, :128]),
                r=[p], w=[self.rowT])
        rf = self.rowflat
        em.emit("q", lambda e: e.dma_start(
            rf.ap.rearrange("p (a b) -> p a b", b=128)[:, :TT, :],
            self.rowT.ap), r=[self.rowT], w=[rf])
        p2 = self.PB
        for tq in range(4):
            em.emit("t", lambda t, q=tq, pp=p2: t.matmul(
                pp.ap[:, :512], self.ones32.ap,
                rf.ap[0:1, q * 512:(q + 1) * 512], start=True, stop=True),
                r=[self.ones32, rf], w=[p2])
            em.emit("s", lambda s, pp=p2, q=tq: s.copy(
                out.ap[:, q * 512:(q + 1) * 512], pp.ap[:, :512]),
                r=[p2], w=[out])

    def _precast_weights(self):
        em = self.em
        st = [self.view(self.F[2], self.F[2].ap[:, :1024], "pcs0"),
              self.view(self.F[2], self.F[2].ap[:, 1024:2048], "pcs1")]
        ob = [self.view(self.B[8], self.B[8].ap[:, :1024], "pcb0"),
              self.view(self.B[9], self.B[9].ap[:, :1024], "pcb1")]
        idx = 0

        def cast_store(src_ap, dst_ap, dstbuf, width, scale=None, reshape_c=None):
            nonlocal idx
            s_ = st[idx % 2]
            o_ = ob[idx % 2]
            idx += 1
            em.emit("g", lambda e, s=src_ap, d=s_: e.dma_start(
                d.ap[:, :width], s), w=[s_], dma=True)
            if scale is None:
                em.emit("v", lambda v, s=s_, o=o_: v.tensor_copy(
                    o.ap[:, :width], s.ap[:, :width]), r=[s_], w=[o_])
            else:
                col, cbuf = scale
                em.emit("v", lambda v, s=s_, o=o_, c=col: v.tensor_scalar(
                    o.ap[:, :width], s.ap[:, :width], c, None, ALU.mult),
                    r=[s_, cbuf], w=[o_])
            src_out = o_.ap[:, :width]
            if reshape_c is not None:
                src_out = src_out.rearrange("p (a c) -> p a c", c=reshape_c)
            em.emit("q", lambda e, d=dst_ap, s=src_out: e.dma_start(
                d, s), r=[o_], w=[dstbuf])

        for k in range(ET):
            dst = self.d_opw16.ap[k].rearrange("a p c -> p a c")
            cast_store(self.i_opw.ap[k * 128:(k + 1) * 128, :], dst,
                       self.d_opw16, D, reshape_c=128)
        for e_ in range(NE):
            for k in range(KD):
                cast_store(self.i_w1.ap[e_, k * 128:(k + 1) * 128, :],
                           self.d_w116.ap[e_, k], self.d_w116, DE)
            for k in range(4):
                cast_store(self.i_w2.ap[e_, k * 128:(k + 1) * 128, :],
                           self.d_w216.ap[e_, k], self.d_w216, D)
        for gi, wsrc in ((0, self.i_gw), (1, self.i_uw)):
            for k in range(KD):
                for h in range(2):
                    cast_store(wsrc.ap[k * 128:(k + 1) * 128,
                                       h * 1024:(h + 1) * 1024],
                               self.d_gud16.ap[gi, k][:, h * 1024:(h + 1) * 1024],
                               self.d_gud16, 1024,
                               scale=(self.fwcol.ap[:, k:k + 1], self.fwcol))
        for k in range(ET):
            cast_store(self.i_dw.ap[k * 128:(k + 1) * 128, :],
                       self.d_dw16.ap[k], self.d_dw16, D)

    def _p25_routing(self):
        em = self.em
        cf, wrf, cnb = self.F[0], self.F[1], self.F[2]
        em.emit("q", lambda e: e.dma_start(cf.ap[:NE, :D], self.i_cent.ap), w=[cf])
        em.emit("q", lambda e: e.dma_start(wrf.ap[:NE, :D], self.i_wroute.ap),
                w=[wrf])
        cn2 = self.view(self.scol, self.scol.ap[:NE, 192:196], "cn2")
        scr8 = self.view(self.F[3], self.F[3].ap[:NE, :D], "scr8")
        em.emit("s", lambda s: s.activation(scr8.ap, cf.ap[:NE, :D], AF.Square,
                accum_out=cn2.ap[:, 0:1]), r=[cf], w=[scr8, cn2])
        em.emit("s", lambda s: s.activation(cn2.ap[:, 1:2], cn2.ap[:, 0:1],
                AF.Sqrt), r=[cn2], w=[cn2])
        em.emit("v", lambda v: v.tensor_scalar(cn2.ap[:, 1:2], cn2.ap[:, 1:2],
                self.consts.ap[:NE, 4:5], None, ALU.max),
                r=[cn2, self.consts], w=[cn2])
        em.emit("v", lambda v: v.reciprocal(cn2.ap[:, 2:3], cn2.ap[:, 1:2]),
                r=[cn2], w=[cn2])
        em.emit("v", lambda v: v.tensor_scalar(cnb.ap[:NE, :D], cf.ap[:NE, :D],
                cn2.ap[:, 2:3], None, ALU.mult), r=[cf, cn2], w=[cnb])
        cnT = self.view(self.B[10], self.B[10].ap[:, :KD * NE], "cnT")
        wrT = self.view(self.B[11], self.B[11].ap[:, :KD * NE], "wrT")
        for (srcb, dstb) in ((cnb, cnT), (wrf, wrT)):
            for k in range(KD):
                em.emit("t", lambda t, s=srcb, kk=k: t.transpose(
                    self.PA.ap[:, :NE], s.ap[:NE, kk * 128:(kk + 1) * 128],
                    self.id32.ap[:NE, :NE]), r=[srcb, self.id32], w=[self.PA])
                em.emit("s", lambda s, d=dstb, kk=k: s.copy(
                    d.ap[:, kk * NE:(kk + 1) * NE], self.PA.ap[:, :NE]),
                    r=[self.PA], w=[dstb])
        for (stat, pp) in ((cnT, self.PA), (wrT, self.PB)):
            for f in range(4):
                for k in range(KD):
                    em.emit("t", lambda t, st=stat, kk=k, ff=f, p_=pp: t.matmul(
                        p_.ap[:NE, ff * 512:(ff + 1) * 512],
                        st.ap[:, kk * NE:(kk + 1) * NE],
                        self.xn[kk].ap[:, ff * 512:(ff + 1) * 512],
                        start=(kk == 0), stop=(kk == KD - 1)),
                        r=[stat, self.xn[k]],
                        w=([pp] if (k == KD - 1 and f == 3) else []),
                        wfirst=([pp] if (k == 0 and f == 0) else []),
                        inc=(f == 3))
        csb, rsb = self.F[0], self.F[1]
        em.emit("v", lambda v: v.tensor_copy(csb.ap[:NE, :T],
                self.PA.ap[:NE, :T]), r=[self.PA], w=[csb])
        em.emit("v", lambda v: v.tensor_copy(rsb.ap[:NE, :T],
                self.PB.ap[:NE, :T]), r=[self.PB], w=[rsb])
        costm, routm = self.F[3], self.F[4]
        for (srcb, dstb) in ((csb, costm), (rsb, routm)):
            for tt in range(TT):
                em.emit("t", lambda t, s=srcb, i=tt: t.transpose(
                    self.PA.ap[:, :NE], s.ap[:NE, i * 128:(i + 1) * 128],
                    self.id32.ap[:NE, :NE]), r=[srcb, self.id32], w=[self.PA])
                em.emit("s", lambda s, d=dstb, i=tt: s.copy(
                    d.ap[:, i * NE:(i + 1) * NE], self.PA.ap[:, :NE]),
                    r=[self.PA], w=[dstb])
        tn = self.view(self.scol, self.scol.ap[:, 196:228], "tnorm")
        em.emit("s", lambda s: s.activation(tn.ap[:, 0:TT],
                self.rms_sumsq.ap, AF.Sqrt), r=[self.rms_sumsq], w=[tn])
        em.emit("v", lambda v: v.tensor_scalar(tn.ap[:, 0:TT], tn.ap[:, 0:TT],
                self.consts.ap[:, 4:5], None, ALU.max),
                r=[tn, self.consts], w=[tn])
        em.emit("v", lambda v: v.reciprocal(tn.ap[:, TT:2 * TT],
                tn.ap[:, 0:TT]), r=[tn], w=[tn])
        gtm = self.F[2]
        sc = self.view(self.scol, self.scol.ap[:, 228:240], "tksc")
        msk = self.view(self.scol, self.scol.ap[:, 240:248], "tkmsk")
        for tt in range(TT):
            ls = costm.ap[:, tt * NE:(tt + 1) * NE]
            gs = gtm.ap[:, tt * NE:(tt + 1) * NE]
            em.emit("v", lambda v, ls=ls, i=tt: v.scalar_tensor_tensor(
                ls, ls, tn.ap[:, TT + i:TT + i + 1],
                routm.ap[:, i * NE:(i + 1) * NE], ALU.mult, ALU.add),
                r=[costm, tn, routm], w=[costm])
            em.emit("v", lambda v, ls=ls: v.reduce_max(sc.ap[:, 0:1], ls,
                    axis=AX), r=[costm], w=[sc])
            em.emit("v", lambda v, ls=ls: v.tensor_scalar(
                msk.ap, ls, sc.ap[:, 0:1], None, ALU.is_equal),
                r=[costm, sc], w=[msk])
            em.emit("v", lambda v, ls=ls: v.scalar_tensor_tensor(
                msk.ap, msk.ap, -1e30, ls, ALU.mult, ALU.add),
                r=[msk, costm], w=[msk])
            em.emit("v", lambda v: v.reduce_max(sc.ap[:, 1:2], msk.ap,
                    axis=AX), r=[msk], w=[sc])
            em.emit("v", lambda v: v.tensor_scalar(sc.ap[:, 2:3],
                    sc.ap[:, 0:1], -1.0, None, ALU.mult), r=[sc], w=[sc])
            em.emit("s", lambda s, ls=ls, gs=gs: s.activation(
                gs, ls, AF.Exp, bias=sc.ap[:, 2:3]), r=[costm, sc], w=[gtm])
            em.emit("v", lambda v, ls=ls: v.tensor_scalar(
                msk.ap, ls, sc.ap[:, 1:2], None, ALU.is_ge),
                r=[costm, sc], w=[msk])
            em.emit("v", lambda v, gs=gs: v.tensor_tensor(gs, gs, msk.ap,
                    ALU.mult), r=[gtm, msk], w=[gtm])
            em.emit("v", lambda v, gs=gs: v.reduce_sum(sc.ap[:, 3:4], gs,
                    axis=AX), r=[gtm], w=[sc])
            em.emit("v", lambda v: v.reciprocal(sc.ap[:, 4:5], sc.ap[:, 3:4]),
                    r=[sc], w=[sc])
            em.emit("v", lambda v, gs=gs: v.tensor_scalar(gs, gs,
                    sc.ap[:, 4:5], None, ALU.mult), r=[gtm, sc], w=[gtm])
        for tt in range(TT):
            em.emit("t", lambda t, i=tt: t.transpose(
                self.PA.ap[:NE, :128], gtm.ap[:, i * NE:(i + 1) * NE],
                self.id32.ap), r=[gtm, self.id32], w=[self.PA])
            em.emit("s", lambda s, i=tt: s.copy(
                self.gatesf.ap[:, i * 128:(i + 1) * 128],
                self.PA.ap[:NE, :128]), r=[self.PA], w=[self.gatesf])
        em.emit("q", lambda e: e.dma_start(self.d_g.ap, self.gatesf.ap),
                r=[self.gatesf], w=[self.d_g])
        self.dump("gatesf", [self.view(self.gatesf, self.gatesf.ap, "g")], T, BF16)

    def _p3_scan(self):
        em = self.em
        dA = [self.F[0], self.F[1]]
        dtq = [self.F[2], self.F[2]]
        inp = [self.B[8], self.B[9]]
        hb = [self.B[10], self.B[11]]
        hc = [self.B[12], self.B[13]]
        uq = [self.B[14], self.B[14]]
        y2 = self.B[16:20]
        bbg = [self.B[21], self.B[22]]
        cbg = [self.B[23], self.B[24]]
        heb = [self.B[25], self.B[25]]
        gb = self.view(self.B[20], self.B[20].ap[:, :TC], "gb")
        xpb = self.view(self.B[20], self.B[20].ap[:, 512:1024], "xpb")
        mb = self.view(self.B[20], self.B[20].ap[:, 1024:1536], "mb")
        hetmp = self.view(self.B[20], self.B[20].ap[:, 1536:2048], "hetmp")
        self.moe_tmp = (mb, hetmp)
        ypsum = self.PA
        bcp = [self.view(self.PB, self.PB.ap[:, 0:512], "bcp0"),
               self.view(self.PB, self.PB.ap[:, 512:1024], "bcp1")]
        auxp = [self.view(self.PB, self.PB.ap[:, 1024:1536], "aux0"),
                self.view(self.PB, self.PB.ap[:, 1536:2048], "aux1")]
        for i in range(2):
            em.emit("g", lambda g, b=self.hstate[i], i=i: g.memset(b.ap, 0.0),
                    w=[self.hstate[i]])

        NGRP = NST // NG

        def stage_rows(c, ng):
            stg = self.brow[ng % 2]
            s4 = stg.ap.rearrange("p (h a b) -> p h a b", h=2, b=TC)
            src4 = bass.AP(self.d_bc.ap.tensor,
                           ng * NG * T + c * TC,
                           [[NST * T, 2], [T, NG], [1, TC]]) \
                if False else None
            em.emit("q", lambda e, n0=ng * NG, cc=c, s4=s4: e.dma_start(
                s4[:, 0, :, :],
                self.d_bc.ap[n0:n0 + NG, cc * TC:(cc + 1) * TC]),
                r=[self.d_bc], w=[stg])
            em.emit("q", lambda e, n0=ng * NG, cc=c, s4=s4: e.dma_start(
                s4[:, 1, :, :],
                self.d_bc.ap[NST + n0:NST + n0 + NG, cc * TC:(cc + 1) * TC]),
                r=[self.d_bc], w=[stg])

        def prebuild(c, ng, bb, cb):
            stg = self.brow[ng % 2]
            s3 = stg.ap.rearrange("p (a b) -> p a b", b=TC)
            for jn in range(NG):
                for (half, dst, pp) in ((0, bb, bcp[0]), (1, cb, bcp[1])):
                    em.emit("t", lambda t, p_=pp, j_=jn, h=half, s3=s3: t.matmul(
                        p_.ap, self.ones16.ap,
                        s3[0:1, h * NG + j_, :], start=True, stop=True),
                        r=[self.ones16, stg], w=[pp])
                    em.emit("s", lambda s, p_=pp, d=dst, j_=jn: s.copy(
                        d.ap[:, j_ * TC:(j_ + 1) * TC], p_.ap),
                        r=[pp], w=[dst])

        moe_done = {c: 0 for c in range(NCH)}

        for c in range(NCH):
            g3 = self.grow.ap.rearrange("p (a b) -> p a b", b=TC)
            em.emit("q", lambda e, cc=c: e.dma_start(
                g3[:, :, :], self.d_g.ap[:, cc * TC:(cc + 1) * TC]),
                r=[self.d_g], w=[self.grow])
            for d in range(KD):
                pp = auxp[d % 2]
                em.emit("t", lambda t, p_=pp, d_=d, cc=c: t.matmul(
                    p_.ap, self.b2sb.ap[:, d_ * 128:(d_ + 1) * 128],
                    self.gatesf.ap[:, cc * TC:(cc + 1) * TC],
                    start=True, stop=True), r=[self.b2sb, self.gatesf], w=[pp])
                em.emit("s", lambda s, p_=pp: s.copy(mb.ap, p_.ap),
                        r=[pp], w=[mb])
                em.emit("g", lambda e, d_=d, cc=c: e.dma_start(
                    self.d_moe.ap[d_][:, cc * TC:(cc + 1) * TC], mb.ap),
                    r=[mb], w=[self.d_moev[d]], dma=True)
            hs_r = self.hstate[c % 2]
            hs_w = self.hstate[(c + 1) % 2]
            hr3 = hs_r.ap.rearrange("p (e n) -> p e n", n=NST)
            hw3 = hs_w.ap.rearrange("p (e n) -> p e n", n=NST)
            for qd in range(4):
                dtb_ = dtq[qd % 2]
                ub_ = uq[qd % 2]
                for j in range(4):
                    et = qd * 4 + j
                    em.emit("q", lambda e, d=dtb_, i=et, jj=j, cc=c: e.dma_start(
                        d.ap[:, jj * 512:(jj + 1) * 512],
                        self.d_dt.ap[i][:, cc * TC:(cc + 1) * TC]),
                        r=[self.d_dt], w=[dtb_])
                    em.emit("q", lambda e, d=ub_, i=et, jj=j, cc=c: e.dma_start(
                        d.ap[:, jj * 512:(jj + 1) * 512],
                        self.d_u.ap[i][:, cc * TC:(cc + 1) * TC]),
                        r=[self.d_u], w=[ub_])
                stage_rows(c, 0)
                stage_rows(c, 1)
                prebuild(c, 0, bbg[0], cbg[0])
                for ng in range(NGRP):
                    if ng + 2 < NGRP:
                        stage_rows(c, ng + 2)
                    bb = bbg[ng % 2]
                    cb = cbg[ng % 2]
                    for jn in range(NG):
                        n = ng * NG + jn
                        ib = n % 2
                        em.emit("s", lambda s, d=dA[ib], src=dtb_, n_=n: s.activation(
                            d.ap, src.ap, AF.Exp,
                            scale=self.asb.ap[:, n_:n_ + 1]),
                            r=[dtb_, self.asb], w=[dA[ib]])
                        bbr = bb.ap[:, jn * TC:(jn + 1) * TC][:, None, :] \
                            .broadcast_to([128, 4, TC])
                        uv = ub_.ap[:, :2048].rearrange("p (a b) -> p a b", b=TC)
                        iv = inp[ib].ap[:, :2048].rearrange("p (a b) -> p a b", b=TC)
                        em.emit("v", lambda v, iv=iv, uv=uv, bbr=bbr: v.tensor_tensor(
                            iv, uv, bbr, ALU.mult), r=[ub_, bb], w=[inp[ib]])
                        for j in range(4):
                            et = qd * 4 + j
                            em.emit("v", lambda v, ib=ib, j_=j, e_=et, n_=n, hr3=hr3: v.tensor_tensor_scan(
                                hb[ib].ap[:, j_ * 512:(j_ + 1) * 512],
                                dA[ib].ap[:, j_ * 512:(j_ + 1) * 512],
                                inp[ib].ap[:, j_ * 512:(j_ + 1) * 512],
                                hr3[:, e_:e_ + 1, n_:n_ + 1], ALU.mult, ALU.add),
                                r=[dA[ib], inp[ib], hs_r],
                                w=([hb[ib]] if j == 3 else []), inc=(j == 3))
                        em.emit("v", lambda v, ib=ib, q_=qd, n_=n, hw3=hw3: v.tensor_copy(
                            hw3[:, 4 * q_:4 * q_ + 4, n_:n_ + 1],
                            hb[ib].ap[:, :2048].rearrange(
                                "p (a b) -> p a b", b=512)[:, :, 511:512]),
                            r=[hb[ib]], w=[hs_w])
                        cbr = cb.ap[:, jn * TC:(jn + 1) * TC][:, None, :] \
                            .broadcast_to([128, 4, TC])
                        hv = hb[ib].ap[:, :2048].rearrange("p (a b) -> p a b", b=TC)
                        hcv = hc[ib].ap[:, :2048].rearrange("p (a b) -> p a b", b=TC)
                        em.emit("v", lambda v, hcv=hcv, hv=hv, cbr=cbr, ib=ib: v.tensor_tensor(
                            hcv, hv, cbr, ALU.mult), r=[hb[ib], cb], w=[hc[ib]])
                        for j in range(4):
                            last = (n == NST - 1)
                            em.emit("t", lambda t, ib=ib, j_=j, n_=n: t.matmul(
                                ypsum.ap[:, j_ * 512:(j_ + 1) * 512], self.id16.ap,
                                hc[ib].ap[:, j_ * 512:(j_ + 1) * 512],
                                start=(n_ == 0), stop=(n_ == NST - 1)),
                                r=[self.id16, hc[ib]],
                                w=([ypsum] if (last and j == 3) else []),
                                wfirst=([ypsum] if (n == 0 and j == 0) else []),
                                inc=(j == 3))
                    if ng + 1 < NGRP:
                        prebuild(c, ng + 1, bbg[(ng + 1) % 2], cbg[(ng + 1) % 2])
                    if ng % 8 == 3 and moe_done[c] < 2 * (qd + 1):
                        self._moe_stage(moe_done[c])
                    if ng % 8 == 7 and moe_done[c] < 2 * (qd + 1):
                        ex = moe_done[c]
                        moe_done[c] += 1
                        self._moe_expert(c, ex, heb, gb, auxp)
                zqv = [gb, xpb, mb, hetmp]
                for j in range(4):
                    et = qd * 4 + j
                    em.emit("q", lambda e, i=et, jj=j, cc=c: e.dma_start(
                        self.B[20].ap[:, jj * 512:(jj + 1) * 512],
                        self.d_z.ap[i][:, cc * TC:(cc + 1) * TC]),
                        r=[self.d_z], w=[zqv[j]])
                em.emit("v", lambda v, q_=qd: v.scalar_tensor_tensor(
                    y2[q_].ap[:, :2048], ypsum.ap, 1.0, self.B[20].ap[:, :2048],
                    ALU.mult, ALU.mult), r=[ypsum] + zqv, w=[y2[qd]])
            while moe_done[c] < NE:
                ex = moe_done[c]
                moe_done[c] += 1
                self._moe_stage(ex)
                self._moe_expert(c, ex, heb, gb, auxp)
            if c == 0:
                self.dump("y2c0", y2, 2048, BF16)
            wst = self.B[15]
            w3 = wst.ap[:, :2048].rearrange("p (a b) -> p a b", b=128)
            for d in range(KD):
                pp = auxp[d % 2]
                em.emit("g", lambda e, d_=d, w3=w3: e.dma_start(
                    w3, self.d_opw16.ap[:, d_].transpose([1, 0, 2])), r=[self.d_opw16],
                    w=[wst], dma=True)
                for k in range(ET):
                    em.emit("t", lambda t, p_=pp, kk=k, w3=w3: t.matmul(
                        p_.ap, w3[:, kk, :],
                        y2[kk // 4].ap[:, (kk % 4) * 512:(kk % 4 + 1) * 512],
                        start=(kk == 0), stop=(kk == ET - 1)),
                        r=[wst, y2[k // 4]],
                        w=([pp] if k == ET - 1 else []),
                        wfirst=([pp] if k == 0 else []), inc=True)
                em.emit("v", lambda v, p_=pp, d_=d, cc=c: v.scalar_tensor_tensor(
                    xpb.ap, p_.ap, 1.0,
                    self.xn[d_].ap[:, cc * TC:(cc + 1) * TC], ALU.mult, ALU.add),
                    r=[pp, self.xn[d]], w=[xpb])
                em.emit("q", lambda e, d_=d, cc=c: e.dma_start(
                    self.d_xp.ap[d_][:, cc * TC:(cc + 1) * TC], xpb.ap),
                    r=[xpb], w=[self.d_xpv[d]])

    def _moe_stage(self, ex):
        em = self.em
        w1st, w2st = self.F[3], self.F[4]
        w1b = w1st.ap.bitcast(BF16).rearrange("p (a b) -> p a b", b=512)
        w2b = w2st.ap.bitcast(BF16).rearrange("p (a b) -> p a b", b=1024)
        em.emit("g", lambda e, e_=ex, w1b=w1b: e.dma_start(
            w1b, self.d_w116.ap[e_].transpose([1, 0, 2])),
            r=[self.d_w116], w=[w1st], dma=True)
        em.emit("g", lambda e, e_=ex, w2b=w2b: e.dma_start(
            w2b, self.d_w216.ap[e_].transpose([1, 0, 2])),
            r=[self.d_w216], w=[w2st], dma=True)

    def _moe_expert(self, c, ex, heb, gb, auxp):
        em = self.em
        he = heb[ex % 2]
        w1st, w2st = self.F[3], self.F[4]
        w1b = w1st.ap.bitcast(BF16).rearrange("p (a b) -> p a b", b=512)
        w2b = w2st.ap.bitcast(BF16).rearrange("p (a b) -> p a b", b=1024)
        g3 = self.grow.ap.rearrange("p (a b) -> p a b", b=TC)
        em.emit("t", lambda t, e_=ex: t.matmul(
            auxp[0].ap, self.ones16.ap, g3[0:1, e_, :],
            start=True, stop=True), r=[self.ones16, self.grow], w=[auxp[0]])
        em.emit("s", lambda s: s.copy(gb.ap, auxp[0].ap), r=[auxp[0]], w=[gb])
        for m in range(4):
            pq = auxp[1]
            for k in range(KD):
                em.emit("t", lambda t, p_=pq, k=k, mm=m, cc=c, w1b=w1b: t.matmul(
                    p_.ap, w1b[:, k, mm * 128:(mm + 1) * 128],
                    self.xn[k].ap[:, cc * TC:(cc + 1) * TC],
                    start=(k == 0), stop=(k == KD - 1)),
                    r=[w1st, self.xn[k]], w=([pq] if k == KD - 1 else []),
                    wfirst=([pq] if k == 0 else []), inc=True)
            hetmp = self.moe_tmp[1]
            em.emit("s", lambda s, p_=pq, e_=ex, mm=m, h_=hetmp: s.activation(
                h_.ap, p_.ap, AF.Gelu,
                bias=self.b1col.ap[:, e_ * 4 + mm:e_ * 4 + mm + 1]),
                r=[pq, self.b1col], w=[hetmp])
            em.emit("v", lambda v, mm=m, h_=hetmp: v.tensor_tensor(
                he.ap[:, mm * TC:(mm + 1) * TC], h_.ap, gb.ap, ALU.mult),
                r=[hetmp, gb], w=[he])
        for d in range(KD):
            pq = auxp[1]
            for k in range(4):
                em.emit("t", lambda t, p_=pq, k=k, d_=d, w2b=w2b: t.matmul(
                    p_.ap, w2b[:, k, d_ * 128:(d_ + 1) * 128],
                    he.ap[:, k * TC:(k + 1) * TC],
                    start=(k == 0), stop=(k == 3)),
                    r=[w2st, he], w=([pq] if k == 3 else []),
                    wfirst=([pq] if k == 0 else []), inc=True)
            mb = self.moe_tmp[0]
            em.emit("s", lambda s, p_=pq, m_=mb: s.copy(m_.ap, p_.ap),
                    r=[pq], w=[mb])
            em.emit("g", lambda e, d_=d, cc=c, m_=mb: e.dma_start(
                self.d_moe.ap[d_][:, cc * TC:(cc + 1) * TC], m_.ap,
                accum_op=ALU.add), r=[mb], w=[self.d_moev[d]], dma=True)

    def _p4_ffn_final(self):
        for hf in range(2):
            self._p4_half(hf)
            self.em.barrier()
        if P4SUB < 8:
            em = self.em
            for tt in range(TT):
                em.emit("q", lambda e, i=tt: e.dma_start(
                    self.o_out.ap[i * 128:(i + 1) * 128, :],
                    self.F[0].ap[:, :D]), r=[self.F[0]], w=[self.o_out])
            em.barrier()

    def _p4_half(self, hf):
        em = self.em
        TH = T // 2
        t0 = hf * TH
        xp = self.B[0:8]
        pre = self.B[8:16]
        fbs = self.B[16:24]

        def fb_ap(m):
            return fbs[m // 2].ap[:, (m % 2) * TH:(m % 2 + 1) * TH]

        def fb_buf(m):
            return fbs[m // 2]

        gtmp = self.view(self.B[24], self.B[24].ap[:, :TH], "gtmp")
        u2t = self.view(self.B[24], self.B[24].ap[:, TH:2 * TH], "u2t")
        rb = self.view(self.B[25], self.B[25].ap[:, :TH], "rb4")
        sqb = [self.view(self.B[25], self.B[25].ap[:, TH:2 * TH], "sq0")] * 2
        f1b = self.F[1].ap.bitcast(BF16)
        xfb = self.view(self.F[1], f1b[:, :TH], "xfb")
        for k in range(KD):
            em.emit("q", lambda e, i=k, b=xp[k], o=t0: e.dma_start(
                b.ap[:, :TH], self.d_xp.ap[i][:, o:o + TH]),
                r=[self.d_xpv[k]], w=[xp[k]])
        prow = self.view(self.PA, self.PA.ap[0:1, :1024], "prow")
        for k in range(KD):
            sq = sqb[k % 2]
            em.emit("s", lambda s, i=k, q=sq: s.activation(
                q.ap, xp[i].ap[:, :TH], AF.Square), r=[xp[k]], w=[sq])
            for f in range(2):
                em.emit("t", lambda t, kk=k, ff=f, q=sq: t.matmul(
                    prow.ap[:, ff * 512:(ff + 1) * 512], self.onecol16.ap,
                    q.ap[:, ff * 512:(ff + 1) * 512],
                    start=(kk == 0), stop=(kk == KD - 1)),
                    r=[self.onecol16, sq],
                    w=([prow, self.PA] if (k == KD - 1 and f == 1) else []),
                    wfirst=([prow, self.PA] if (k == 0 and f == 0) else []),
                    inc=(f == 1))
        sr = self.srow
        em.emit("v", lambda v: v.tensor_scalar(sr.ap[:, :TH], prow.ap[:, :TH],
                1.0 / D, None, ALU.mult), r=[prow], w=[sr])
        em.emit("s", lambda s: s.activation(sr.ap[:, TH:2 * TH], sr.ap[:, :TH],
                AF.Sqrt, bias=self.consts.ap[:1, 0:1]),
                r=[sr, self.consts], w=[sr])
        em.emit("v", lambda v: v.reciprocal(sr.ap[:, 2 * TH:3 * TH],
                sr.ap[:, TH:2 * TH]), r=[sr], w=[sr])
        if P4SUB < 2:
            return
        em.barrier()
        for f in range(2):
            em.emit("t", lambda t, ff=f: t.matmul(
                self.PA.ap[:, ff * 512:(ff + 1) * 512], self.ones32.ap,
                sr.ap[:, 2 * TH + ff * 512:2 * TH + (ff + 1) * 512],
                start=True, stop=True), r=[self.ones32, sr],
                w=([self.PA] if f == 1 else []),
                wfirst=([self.PA] if f == 0 else []), inc=(f == 1))
        em.emit("s", lambda s: s.copy(rb.ap, self.PA.ap[:, :TH]),
                r=[self.PA], w=[rb])
        for d in range(KD):
            t1 = sqb[0]
            em.emit("q", lambda e, d_=d, b=xfb, o=t0: e.dma_start(
                b.ap, self.d_xf.ap[d_][:, o:o + TH]), r=[self.d_xf], w=[xfb])
            em.emit("v", lambda v, d_=d, b=xfb, t1=t1: v.tensor_tensor(
                t1.ap, b.ap, xp[d_].ap[:, :TH], ALU.add),
                r=[xfb, xp[d]], w=[t1])
            em.emit("q", lambda e, d_=d, b=xfb, o=t0: e.dma_start(
                b.ap, self.d_moe.ap[d_][:, o:o + TH]),
                r=[self.d_moev[d]], w=[xfb])
            em.emit("v", lambda v, d_=d, b=xfb, t1=t1: v.tensor_tensor(
                pre[d_].ap[:, :TH], t1.ap, b.ap, ALU.add),
                r=[xfb, t1], w=[pre[d]])
        if P4SUB < 7:
            return
        gust = self.F[3]
        gub = gust.ap.bitcast(BF16).rearrange("p (a b) -> p a b", b=128)
        for m in range(ET):
            pg, pu = self.PA, self.PB
            em.emit("g", lambda e, mm=m, gub=gub: e.dma_start(
                gub[:, 0:8, :], self.d_gud16.ap[0, :, :, mm * 128:(mm + 1) * 128].transpose([1, 0, 2])),
                r=[self.d_gud16], w=[gust], dma=True)
            em.emit("g", lambda e, mm=m, gub=gub: e.dma_start(
                gub[:, 8:16, :], self.d_gud16.ap[1, :, :, mm * 128:(mm + 1) * 128].transpose([1, 0, 2])),
                r=[self.d_gud16], w=[gust], dma=True)
            for k in range(KD):
                for f in range(2):
                    em.emit("t", lambda t, pp=pg, kk=k, ff=f, gub=gub: t.matmul(
                        pp.ap[:, ff * 512:(ff + 1) * 512], gub[:, kk, :],
                        xp[kk].ap[:, ff * 512:(ff + 1) * 512],
                        start=(kk == 0), stop=(kk == KD - 1)),
                        r=[gust, xp[k]],
                        w=([pg] if (k == KD - 1 and f == 1) else []),
                        wfirst=([pg] if (k == 0 and f == 0) else []),
                        inc=(f == 1))
                    em.emit("t", lambda t, pp=pu, kk=k, ff=f, gub=gub: t.matmul(
                        pp.ap[:, ff * 512:(ff + 1) * 512], gub[:, 8 + kk, :],
                        xp[kk].ap[:, ff * 512:(ff + 1) * 512],
                        start=(kk == 0), stop=(kk == KD - 1)),
                        r=[gust, xp[k]],
                        w=([pu] if (k == KD - 1 and f == 1) else []),
                        wfirst=([pu] if (k == 0 and f == 0) else []),
                        inc=(f == 1))
            em.emit("v", lambda v: v.tensor_tensor(gtmp.ap,
                    self.PA.ap[:, :TH], rb.ap, ALU.mult),
                    r=[self.PA, rb], w=[gtmp])
            em.emit("s", lambda s: s.activation(xfb.ap, gtmp.ap, AF.Silu),
                    r=[gtmp], w=[xfb])
            em.emit("v", lambda v: v.tensor_tensor(u2t.ap, self.PB.ap[:, :TH],
                    rb.ap, ALU.mult), r=[self.PB, rb], w=[u2t])
            em.emit("v", lambda v, i=m: v.tensor_tensor(fb_ap(i), xfb.ap,
                    u2t.ap, ALU.mult), r=[xfb, u2t], w=[fb_buf(m)])
        if P4SUB < 8:
            return
        otm = [self.view(self.F[2], self.F[2].ap[:, :512], "otm0"),
               self.view(self.F[2], self.F[2].ap[:, 512:1024], "otm1")]
        ofm = self.F[0]
        dwst = self.F[4]
        dwb = dwst.ap.bitcast(BF16).rearrange("p (a b) -> p a b", b=128)
        for d in range(KD):
            pp = self.PA if d % 2 == 0 else self.PB
            em.emit("g", lambda e, d_=d, dwb=dwb: e.dma_start(
                dwb[:, 0:16, :], self.d_dw16.ap[:, :, d_ * 128:(d_ + 1) * 128].transpose([1, 0, 2])),
                r=[self.d_dw16], w=[dwst], dma=True)
            for k in range(ET):
                for f in range(2):
                    em.emit("t", lambda t, pp=pp, kk=k, ff=f, dwb=dwb: t.matmul(
                        pp.ap[:, ff * 512:(ff + 1) * 512], dwb[:, kk, :],
                        fb_ap(kk)[:, ff * 512:(ff + 1) * 512],
                        start=(kk == 0), stop=(kk == ET - 1)),
                        r=[dwst, fb_buf(k)],
                        w=([pp] if (k == ET - 1 and f == 1) else []),
                        wfirst=([pp] if (k == 0 and f == 0) else []),
                        inc=(f == 1))
            em.emit("v", lambda v, pp=pp, d_=d: v.scalar_tensor_tensor(
                ofm.ap[:, :TH], pp.ap[:, :TH], 1.0, pre[d_].ap[:, :TH],
                ALU.mult, ALU.add), r=[pp, pre[d]], w=[ofm])
            po = self.PB if d % 2 == 0 else self.PA
            for tq in range(2):
                o = otm[tq % 2]
                for j in range(4):
                    tt = tq * 4 + j
                    em.emit("t", lambda t, pp=po, jj=j, i=tt: t.transpose(
                        pp.ap[:, 1024 + jj * 128:1024 + (jj + 1) * 128],
                        ofm.ap[:, i * 128:(i + 1) * 128], self.id32.ap),
                        r=[ofm, self.id32], w=([po] if j == 3 else []),
                        wfirst=([po] if j == 0 else []), inc=(j == 3))
                em.emit("v", lambda v, pp=po, o=o: v.tensor_copy(
                    o.ap, pp.ap[:, 1024:1536]), r=[po], w=[o])
                for j in range(4):
                    tt = tq * 4 + j
                    em.emit("q", lambda e, o=o, d_=d, i=tt, jj=j, o0=t0: e.dma_start(
                        self.o_out.ap[o0 + i * 128:o0 + (i + 1) * 128,
                                      d_ * 128:(d_ + 1) * 128],
                        o.ap[:, jj * 128:(jj + 1) * 128]),
                        r=[o], w=[self.o_out])


def build_program(dbg=()):
    kb = KB(dbg=dbg)
    nc = kb.build()
    return nc


def _in_maps(inputs):
    f32 = np.float32
    x = np.ascontiguousarray(np.asarray(inputs["L0_attn_x"], dtype=f32))

    def g(k):
        return np.ascontiguousarray(np.asarray(inputs[k], dtype=f32))

    common = {
        "inw": g("L0_attn_in_proj_W"),
        "convw": g("L0_attn_conv_W").reshape(4, E),
        "convb": g("L0_attn_conv_b"),
        "dtw": g("L0_attn_dt_W"),
        "dtb": g("L0_attn_dt_b"),
        "bcw": np.ascontiguousarray(np.concatenate(
            [g("L0_attn_B_W"), g("L0_attn_C_W")], axis=1)),
        "alog": g("L0_attn_A_log").reshape(1, NST),
        "opw": g("L0_attn_out_proj_W"),
        "gw": g("L0_attn_gate_W"),
        "uw": g("L0_attn_up_W"),
        "dw": g("L0_attn_down_W"),
        "normw": g("L0_attn_input_norm_w"),
        "fnormw": g("L0_attn_ffn_norm_w"),
        "w1": g("L0_ffn_w1"),
        "b1": g("L0_ffn_b1"),
        "w2": g("L0_ffn_w2"),
        "b2": g("L0_ffn_b2"),
        "cent": g("L0_ffn_centroids"),
        "wroute": g("L0_ffn_w_route"),
    }
    return [dict(common, x=np.ascontiguousarray(x[i])) for i in range(N_CORES)]


def kernel(**inputs):
    nc = build_program(dbg=DBG)
    res = run_bass_kernel_spmd(nc, _in_maps(inputs), core_ids=list(range(N_CORES)))
    out = np.stack([res.results[i]["out"] for i in range(N_CORES)], axis=0)
    return np.asarray(out, dtype=np.float32)


# version 06 (final)
# speedup vs baseline: 1.1049x; 1.0347x over previous
import numpy as np
import ml_dtypes
from contextlib import ExitStack

import concourse.bass as bass
import concourse.mybir as mybir
from concourse.bass_utils import run_bass_kernel_spmd

F32 = mybir.dt.float32
BF16 = mybir.dt.bfloat16
AF = mybir.ActivationFunctionType
ALU = mybir.AluOpType
AX = mybir.AxisListType.X

T = 2048
D = 1024
E = 2048
NST = 64
ET = 16
KD = 8
NCH = 4
TC = 512
NE = 8
DE = 512
TT = 16
NG = 4

DBG = []
PHASES = 5
P4SUB = 9
N_CORES = 8


class Buf:
    __slots__ = ("ap", "w", "r", "name")

    def __init__(self, ap, name=""):
        self.ap = ap
        self.w = None
        self.r = {}
        self.name = name


class EM:
    ENGS = ("v", "s", "t", "g", "q")

    def __init__(self, nc, ctx):
        self.nc = nc
        self.plan = {e: [] for e in self.ENGS}
        self.sems = {k: ctx.enter_context(nc.semaphore(f"em_{k}"))
                     for k in ("v", "s", "t", "g", "qd", "gd", "qp", "gp")}
        self.count = {k: 0 for k in self.sems}
        self.mark = {e: {} for e in self.ENGS}
        self.pub_covered = {"qd": 0, "gd": 0}

    def _wait(self, eng, key, cnt):
        if cnt <= 0:
            return
        if key in ("qd", "gd"):
            issuer = "q" if key == "qd" else "g"
            pubkey = "qp" if key == "qd" else "gp"
            if eng == issuer:
                cnt = self.count[key]
            else:
                if self.pub_covered[key] < cnt:
                    total = self.count[key]
                    self._wait(issuer, key, total)
                    self.count[pubkey] += 1
                    sem = self.sems[pubkey]
                    self.plan[issuer].append(
                        lambda e, sem=sem: e.nop().then_inc(sem, 1))
                    self.pub_covered[key] = total
                self._wait(eng, pubkey, self.count[pubkey])
                return
        if self.mark[eng].get(key, 0) >= cnt:
            return
        self.mark[eng][key] = cnt
        sem = self.sems[key]
        val = cnt * 16 if key in ("qd", "gd") else cnt
        self.plan[eng].append(lambda e, sem=sem, val=val: e.wait_ge(sem, val))

    def emit(self, eng, fn, r=(), w=(), inc=True, dma=False, wfirst=()):
        if eng == "q":
            semkey, dma = "qd", True
        elif eng == "g" and dma:
            semkey = "gd"
        else:
            semkey = eng
        for b in r:
            if b.w is not None:
                self._wait(eng, *b.w)
        for b in list(w) + list(wfirst):
            if b.w is not None:
                self._wait(eng, *b.w)
            for key, cnt in b.r.items():
                self._wait(eng, key, cnt)
        if inc:
            self.count[semkey] += 1
            cnt = self.count[semkey]
            sem = self.sems[semkey]
            amt = 16 if semkey in ("qd", "gd") else 1
            self.plan[eng].append(
                lambda e, fn=fn, sem=sem, amt=amt: fn(e).then_inc(sem, amt))
        else:
            cnt = self.count[semkey] + 1
            self.plan[eng].append(lambda e, fn=fn: fn(e))
        for b in r:
            if b.r.get(semkey, 0) < cnt:
                b.r[semkey] = cnt
        for b in w:
            b.w = (semkey, cnt)
            b.r = {}
        return cnt

    def barrier(self):
        for eng in self.ENGS:
            for key in self.sems:
                self._wait(eng, key, self.count[key])

    def replay(self, block):
        block.vector(lambda e: [f(e) for f in self.plan["v"]])
        block.scalar(lambda e: [f(e) for f in self.plan["s"]])
        block.tensor(lambda e: [f(e) for f in self.plan["t"]])
        block.gpsimd(lambda e: [f(e) for f in self.plan["g"]])
        block.sync(lambda e: [f(e) for f in self.plan["q"]])


class KB:
    def __init__(self, dbg=()):
        self.nc = bass.Bass("TRN2", target_bir_lowering=False)
        self.ctx = ExitStack()
        self.em = EM(self.nc, self.ctx)
        self.dbg_names = set(dbg)
        self._uid = 0

    def sb(self, name, shape, dtype):
        h = self.ctx.enter_context(self.nc.sbuf_tensor(name, shape, dtype))
        return Buf(h.ap(), name)

    def dram(self, name, shape, dtype, kind="Internal"):
        return Buf(self.nc.dram_tensor(name, shape, dtype, kind=kind).ap(), name)

    def view(self, buf, ap, name=""):
        return Buf(ap, name or buf.name + "/v")

    def dump(self, name, bufs, width=T, dtype=F32):
        if name not in self.dbg_names:
            return
        d = self.dram("dbg_" + name, [len(bufs), 128, width], dtype,
                      kind="ExternalOutput")
        for i, b in enumerate(bufs):
            self.em.emit("q", lambda e, i=i, b=b: e.dma_start(
                d.ap[i], b.ap[:, :width]), r=[b], w=[d])

    def build(self):
        em = self.em
        self.i_x = self.dram("x", [T, D], F32, kind="ExternalInput")
        self.i_inw = self.dram("inw", [D, 2 * E], F32, kind="ExternalInput")
        self.i_convw = self.dram("convw", [4, E], F32, kind="ExternalInput")
        self.i_convb = self.dram("convb", [E], F32, kind="ExternalInput")
        self.i_dtw = self.dram("dtw", [E, E], F32, kind="ExternalInput")
        self.i_dtb = self.dram("dtb", [E], F32, kind="ExternalInput")
        self.i_bcw = self.dram("bcw", [E, 2 * NST], F32, kind="ExternalInput")
        self.i_alog = self.dram("alog", [1, NST], F32, kind="ExternalInput")
        self.i_opw = self.dram("opw", [E, D], F32, kind="ExternalInput")
        self.i_gw = self.dram("gw", [D, E], F32, kind="ExternalInput")
        self.i_uw = self.dram("uw", [D, E], F32, kind="ExternalInput")
        self.i_dw = self.dram("dw", [E, D], F32, kind="ExternalInput")
        self.i_normw = self.dram("normw", [D], F32, kind="ExternalInput")
        self.i_fnormw = self.dram("fnormw", [D], F32, kind="ExternalInput")
        self.i_w1 = self.dram("w1", [NE, D, DE], F32, kind="ExternalInput")
        self.i_b1 = self.dram("b1", [NE, DE], F32, kind="ExternalInput")
        self.i_w2 = self.dram("w2", [NE, DE, D], F32, kind="ExternalInput")
        self.i_b2 = self.dram("b2", [NE, D], F32, kind="ExternalInput")
        self.i_cent = self.dram("cent", [NE, D], F32, kind="ExternalInput")
        self.i_wroute = self.dram("wroute", [NE, D], F32, kind="ExternalInput")
        self.o_out = self.dram("out", [T, D], F32, kind="ExternalOutput")

        self.d_dt = self.dram("d_dt", [ET, 128, T], F32)
        self.d_u = self.dram("d_u", [ET, 128, T], BF16)
        self.d_z = self.dram("d_z", [ET, 128, T], BF16)
        self.d_xf = self.dram("d_xf", [KD, 128, T], BF16)
        self.d_xp = self.dram("d_xp", [KD, 128, T], BF16)
        self.d_moe = self.dram("d_moe", [KD, 128, T], BF16)
        self.d_xpv = [self.view(self.d_xp, self.d_xp.ap[i], f"d_xp{i}")
                      for i in range(KD)]
        self.d_moev = [self.view(self.d_moe, self.d_moe.ap[i], f"d_moe{i}")
                       for i in range(KD)]
        self.d_opw16 = self.dram("d_opw16", [ET, KD, 128, 128], BF16)
        self.d_w116 = self.dram("d_w116", [NE, KD, 128, DE], BF16)
        self.d_w216 = self.dram("d_w216", [NE, 4, 128, D], BF16)
        self.d_gud16 = self.dram("d_gud16", [2, KD, 128, E], BF16)
        self.d_dw16 = self.dram("d_dw16", [ET, 128, D], BF16)
        self.d_bc = self.dram("d_bc", [128, T], BF16)
        self.d_g = self.dram("d_g", [NE, T], BF16)

        self.F = [self.sb(f"fa{i}", [128, 2048], F32) for i in range(5)]
        self.B = [self.sb(f"ba{i}", [128, 2052], BF16) for i in range(26)]
        self.xn = self.B[0:8]

        self.ones16 = self.sb("ones16", [1, 128], BF16)
        self.ones32 = self.sb("ones32", [1, 128], F32)
        self.onecol16 = self.sb("onecol16", [128, 1], BF16)
        self.id16 = self.sb("id16", [128, 128], BF16)
        self.id32 = self.sb("id32", [128, 128], F32)
        self.consts = self.sb("consts", [128, 8], F32)
        self.scol = self.sb("scol", [128, 512], F32)
        self.wcol = self.sb("wcol", [128, KD], F32)
        self.fwcol = self.sb("fwcol", [128, KD], F32)
        self.cbcol = self.sb("cbcol", [128, ET], F32)
        self.dbcol = self.sb("dbcol", [128, ET], F32)
        self.cwcol = self.sb("cwcol", [128, ET * 4], F32)
        self.b1col = self.sb("b1col", [128, NE * 4], F32)
        self.asb = self.sb("asb", [128, NST], F32)
        self.arow = self.sb("arow", [1, NST], F32)
        self.bcsb = self.sb("bcsb", [128, T], BF16)
        self.gatesf = self.sb("gatesf", [NE, T], BF16)
        self.b2sb = self.sb("b2sb", [NE, D], BF16)
        self.hstate = [self.sb(f"hst{i}", [128, ET * NST], F32) for i in range(2)]
        self.rowT = self.sb("rowT", [TT, 128], F32)
        self.brow = [self.sb(f"brow{i}", [1, 2 * NG * TC], BF16) for i in range(2)]
        self.grow = self.sb("grow", [1, NE * TC], BF16)
        self.srow = self.sb("srow", [1, 3 * 1024], F32)
        self.rowflat = Buf(self.srow.ap[:, :2048], "rowflat")
        self.wf32 = [self.sb(f"wf{i}", [128, 128], F32) for i in range(2)]
        self.wb16 = [self.sb(f"wb{i}", [128, 128], BF16) for i in range(8)]

        PA = self.ctx.enter_context(self.nc.psum_tensor("PA", [128, 2048], F32))
        PB = self.ctx.enter_context(self.nc.psum_tensor("PB", [128, 2048], F32))
        self.PA = Buf(PA.ap(), "PA")
        self.PB = Buf(PB.ap(), "PB")

        self._p0_consts()
        if PHASES >= 1:
            self._p1_load_ln()
            em.barrier()
        if PHASES >= 2:
            self._p2_mixer_pre()
            em.barrier()
        if PHASES >= 3:
            self._p25_routing()
            em.barrier()
        if PHASES >= 4:
            self._p3_scan()
            em.barrier()
        if PHASES >= 5:
            self._p4_ffn_final()
            em.barrier()
        else:
            for tt in range(TT):
                em.emit("q", lambda e, i=tt: e.dma_start(
                    self.o_out.ap[i * 128:(i + 1) * 128, :],
                    self.F[0].ap[:, :D]), r=[self.F[0]], w=[self.o_out])
            em.barrier()

        for nm in list(self.dbg_names):
            if not nm.startswith("d_"):
                continue
            t = getattr(self, nm, None)
            if t is None:
                continue
            sh = list(t.ap.shape)
            o = self.dram("dbgo_" + nm, sh, t.ap.dtype, kind="ExternalOutput")
            em.emit("q", lambda e, o=o, t=t: e.dma_start(o.ap, t.ap),
                    r=[t], w=[o])
        em.barrier()

        block = self.ctx.enter_context(self.nc.Block())
        em.replay(block)
        self.ctx.close()
        return self.nc

    def _p0_consts(self):
        em = self.em
        for b, val in ((self.ones16, 1.0), (self.ones32, 1.0), (self.onecol16, 1.0)):
            em.emit("g", lambda g, b=b, v=val: g.memset(b.ap, v), w=[b])
        for ident in (self.id16, self.id32):
            em.emit("g", lambda g, a=ident.ap: g.memset(a, 0.0), w=[ident])
            em.emit("g", lambda g, a=ident.ap: g.affine_select(
                out=a, in_=a, compare_op=ALU.not_equal, fill=1.0,
                base=0, pattern=[[-1, 128]], channel_multiplier=1),
                r=[ident], w=[ident])
        for col, val in ((0, 1e-5), (1, 1.0 / D), (2, -0.5), (3, 1.5), (4, 1e-12)):
            em.emit("g", lambda g, c=col, v=val: g.memset(
                self.consts.ap[:, c:c + 1], v), w=[self.consts])

        def vload(dst, src_ap, ncols):
            v2 = src_ap.rearrange("(a b) -> a b", b=128).transpose([1, 0])
            em.emit("q", lambda e, d=dst, s=v2, n=ncols: e.dma_start(
                d.ap[:, :n], s, allow_slow_non_contiguous=True), w=[dst])

        vload(self.wcol, self.i_normw.ap, KD)
        vload(self.fwcol, self.i_fnormw.ap, KD)
        vload(self.cbcol, self.i_convb.ap, ET)
        vload(self.dbcol, self.i_dtb.ap, ET)
        for wd in range(4):
            cv = self.i_convw.ap[wd].rearrange("(i b) -> b i", b=128)
            cwd = self.cwcol.ap.rearrange("p (i w) -> p i w", w=4)[:, :, wd]
            em.emit("q", lambda e, cv=cv, cwd=cwd: e.dma_start(cwd, cv,
                    allow_slow_non_contiguous=True), w=[self.cwcol])
        for e_ in range(NE):
            b1v = self.i_b1.ap[e_].rearrange("(m b) -> b m", b=128)
            b1d = self.b1col.ap[:, e_ * 4:(e_ + 1) * 4]
            em.emit("q", lambda e, b1v=b1v, b1d=b1d: e.dma_start(b1d, b1v,
                    allow_slow_non_contiguous=True), w=[self.b1col])
        em.emit("q", lambda e: e.dma_start(self.arow.ap, self.i_alog.ap), w=[self.arow])
        em.emit("s", lambda s: s.activation(self.arow.ap, self.arow.ap, AF.Exp),
                r=[self.arow], w=[self.arow])
        em.emit("v", lambda v: v.tensor_scalar(self.arow.ap, self.arow.ap, -1.0,
                None, ALU.mult), r=[self.arow], w=[self.arow])
        em.emit("t", lambda t: t.matmul(self.PA.ap[:, :NST], self.ones32.ap,
                self.arow.ap, start=True, stop=True),
                r=[self.ones32, self.arow], w=[self.PA])
        em.emit("s", lambda s: s.copy(self.asb.ap, self.PA.ap[:, :NST]),
                r=[self.PA], w=[self.asb])
        b2f = self.F[4]
        em.emit("q", lambda e: e.dma_start(b2f.ap[:NE, :D], self.i_b2.ap), w=[b2f])
        em.emit("v", lambda v: v.tensor_copy(self.b2sb.ap, b2f.ap[:NE, :D]),
                r=[b2f], w=[self.b2sb])

    def _rsqrt(self, src_ap, dst_ap, rbufs, wbufs, scr_ap, scrbuf, eps_col=0):
        em = self.em
        n = src_ap.shape[-1]
        em.emit("s", lambda s: s.activation(scr_ap[:, :n], src_ap, AF.Sqrt,
                bias=self.consts.ap[:src_ap.shape[0], eps_col:eps_col + 1]),
                r=rbufs + [self.consts], w=[scrbuf])
        em.emit("v", lambda v: v.reciprocal(dst_ap, scr_ap[:, :n]),
                r=[scrbuf], w=wbufs)

    def _wtile(self, src_ap, idx, scale_col=None):
        em = self.em
        f = self.wf32[idx % 2]
        b = self.wb16[idx % 8]
        em.emit("g", lambda e, s=src_ap, d=f.ap: e.dma_start(d, s), w=[f], dma=True)
        if scale_col is None:
            em.emit("v", lambda v, s=f.ap, d=b.ap: v.tensor_copy(d, s), r=[f], w=[b])
        else:
            col, cbuf = scale_col
            em.emit("v", lambda v, s=f.ap, d=b.ap, c=col: v.tensor_scalar(
                d, s, c, None, ALU.mult), r=[f, cbuf], w=[b])
        return b

    def _bcast_row(self, src_row_ap, rbufs, dst_ap, wbuf, psum, width, f32=False):
        em = self.em
        ones = self.ones32 if f32 else self.ones16
        em.emit("t", lambda t: t.matmul(psum.ap[:, :width], ones.ap, src_row_ap,
                start=True, stop=True), r=[ones] + rbufs, w=[psum])
        em.emit("s", lambda s: s.copy(dst_ap, psum.ap[:, :width]),
                r=[psum], w=[wbuf])

    def _p1_load_ln(self):
        em = self.em
        sums = self.view(self.scol, self.scol.ap[:, 0:32], "lnsums")
        mcol = self.view(self.scol, self.scol.ap[:, 32:48], "mcol")
        varc = self.view(self.scol, self.scol.ap[:, 48:64], "varc")
        rstd = self.view(self.scol, self.scol.ap[:, 64:80], "rstd")
        nsc = self.view(self.scol, self.scol.ap[:, 80:112], "nsc")
        rmss = self.view(self.scol, self.scol.ap[:, 112:128], "rmss")
        self.rms_sumsq = rmss
        self.ln_cols = (mcol, rstd)
        scrt = self.F[4]
        xf = self.B[8:16]
        for tt in range(TT):
            sl = self.F[tt % 4]
            em.emit("q", lambda e, d=sl, i=tt: e.dma_start(
                d.ap[:, :D], self.i_x.ap[i * 128:(i + 1) * 128, :]), w=[sl])
            em.emit("s", lambda s, x=sl, i=tt: s.activation(
                scrt.ap[:, :D], x.ap[:, :D], AF.Copy,
                accum_out=sums.ap[:, i:i + 1]), r=[sl], w=[scrt, sums])
            em.emit("s", lambda s, x=sl, i=tt: s.activation(
                scrt.ap[:, :D], x.ap[:, :D], AF.Square,
                accum_out=sums.ap[:, 16 + i:17 + i]), r=[sl], w=[scrt, sums])
            em.emit("v", lambda v, i=tt: v.tensor_scalar(
                mcol.ap[:, i:i + 1], sums.ap[:, i:i + 1], 1.0 / D, None,
                ALU.mult), r=[sums], w=[mcol])
            em.emit("v", lambda v, i=tt: v.tensor_tensor(
                varc.ap[:, i:i + 1], mcol.ap[:, i:i + 1], mcol.ap[:, i:i + 1],
                ALU.mult), r=[mcol], w=[varc])
            em.emit("v", lambda v, i=tt: v.scalar_tensor_tensor(
                varc.ap[:, i:i + 1], sums.ap[:, 16 + i:17 + i], 1.0 / D,
                varc.ap[:, i:i + 1], ALU.mult, ALU.subtract),
                r=[sums, varc], w=[varc])
            self._rsqrt(varc.ap[:, tt:tt + 1], rstd.ap[:, tt:tt + 1],
                        [varc], [rstd], nsc.ap[:, 0:2], nsc)
            self._transpose_tile(sl, tt, xf, F32)
            em.emit("v", lambda v, x=sl, i=tt: v.tensor_scalar(
                x.ap[:, :D], x.ap[:, :D], mcol.ap[:, i:i + 1],
                rstd.ap[:, i:i + 1], ALU.subtract, ALU.mult),
                r=[sl, mcol, rstd], w=[sl])
            em.emit("s", lambda s, x=sl, i=tt: s.activation(
                scrt.ap[:, :D], x.ap[:, :D], AF.Square,
                accum_out=rmss.ap[:, i:i + 1]), r=[sl], w=[scrt, rmss])
            self._transpose_tile(sl, tt, self.xn, F32)
        for k in range(KD):
            em.emit("q", lambda e, i=k, b=xf[k]: e.dma_start(
                self.d_xf.ap[i], b.ap[:, :T]), r=[xf[k]], w=[self.d_xf])
        self.dump("xn", self.xn, T, BF16)

    def _transpose_tile(self, tile, tt, fm_tiles, in_dtype):
        em = self.em
        ident = self.id32 if in_dtype == F32 else self.id16
        for kq in range(KD // 4):
            p = self.PA if kq % 2 == 0 else self.PB
            for j in range(4):
                k = kq * 4 + j
                em.emit("t", lambda t, x=tile, kk=k, pp=p, jj=j: t.transpose(
                    pp.ap[:, jj * 128:(jj + 1) * 128],
                    x.ap[:, kk * 128:(kk + 1) * 128], ident.ap),
                    r=[tile, ident], w=([p] if j == 3 else []),
                    wfirst=([p] if j == 0 else []), inc=(j == 3))
            for j in range(4):
                k = kq * 4 + j
                em.emit("s", lambda s, kk=k, pp=p, jj=j, i=tt: s.copy(
                    fm_tiles[kk].ap[:, i * 128:(i + 1) * 128],
                    pp.ap[:, jj * 128:(jj + 1) * 128]), r=[p], w=[fm_tiles[k]])

    def _p2_mixer_pre(self):
        em = self.em
        rmsr = self.view(self.scol, self.scol.ap[:, 128:144], "rmsr")
        nsc = self.view(self.scol, self.scol.ap[:, 144:176], "nsc2")
        mean = self.view(self.scol, self.scol.ap[:, 176:192], "rmsmean")
        em.emit("v", lambda v: v.tensor_scalar(mean.ap, self.rms_sumsq.ap,
                1.0 / D, None, ALU.mult), r=[self.rms_sumsq], w=[mean])
        self._rsqrt(mean.ap, rmsr.ap, [mean], [rmsr], nsc.ap, nsc)
        rb = self.F[4]
        self._bcast_cols(rmsr, rb)
        self.dump("rmsr", [rb], T, BF16)

        xz = self.B[8:10]
        xc = self.B[10:26]
        zbuf = self.B[9]
        wi = 0
        ztmp = self.F[2]
        for m in range(32):
            p = self.PA if m % 2 == 0 else self.PB
            order_m = m + 16 if m < 16 else m - 16
            for k in range(KD):
                wb = self._wtile(self.i_inw.ap[k * 128:(k + 1) * 128,
                                               order_m * 128:(order_m + 1) * 128],
                                 wi, scale_col=(self.wcol.ap[:, k:k + 1], self.wcol))
                wi += 1
                for f in range(4):
                    em.emit("t", lambda t, pp=p, w=wb, kk=k, ff=f: t.matmul(
                        pp.ap[:, ff * 512:(ff + 1) * 512], w.ap,
                        self.xn[kk].ap[:, ff * 512:(ff + 1) * 512],
                        start=(kk == 0), stop=(kk == KD - 1)),
                        r=[wb, self.xn[k]],
                        w=([p] if (k == KD - 1 and f == 3) else []),
                        wfirst=([p] if (k == 0 and f == 0) else []),
                        inc=True)
            if order_m >= ET:
                i = order_m - ET
                em.emit("v", lambda v, pp=p: v.tensor_tensor(
                    ztmp.ap, pp.ap, rb.ap[:, :T], ALU.mult), r=[p, rb], w=[ztmp])
                em.emit("s", lambda s: s.activation(zbuf.ap[:, :T], ztmp.ap,
                        AF.Silu), r=[ztmp], w=[zbuf])
                em.emit("q", lambda e, i=i: e.dma_start(self.d_z.ap[i],
                        zbuf.ap[:, :T]), r=[zbuf], w=[self.d_z])
            else:
                i = order_m
                sl = xz[i % 2]
                em.emit("g", lambda g, b=sl: g.memset(b.ap[:, 0:1], 0.0), w=[sl])
                em.emit("g", lambda g, b=sl: g.memset(b.ap[:, T + 1:T + 3], 0.0),
                        r=[sl], w=[sl])
                em.emit("v", lambda v, pp=p, b=sl: v.tensor_tensor(
                    b.ap[:, 1:T + 1], pp.ap, rb.ap[:, :T], ALU.mult),
                    r=[p, rb], w=[sl])
                acc = [self.F[0], self.F[1]]
                em.emit("v", lambda v, i=i, a=acc[0], b=sl: v.tensor_scalar(
                    a.ap, b.ap[:, 0:T], self.cwcol.ap[:, i * 4:i * 4 + 1], None,
                    ALU.mult), r=[sl, self.cwcol], w=[acc[0]])
                for wd in range(1, 4):
                    src_, dst_ = acc[(wd + 1) % 2], acc[wd % 2]
                    em.emit("v", lambda v, i=i, s_=src_, d_=dst_, b=sl, wd=wd:
                            v.scalar_tensor_tensor(
                        d_.ap, b.ap[:, wd:wd + T],
                        self.cwcol.ap[:, i * 4 + wd:i * 4 + wd + 1], s_.ap,
                        ALU.mult, ALU.add), r=[sl, self.cwcol, src_], w=[dst_])
                em.emit("s", lambda s, i=i, a=acc[1]: s.activation(
                    xc[i].ap[:, :T], a.ap, AF.Silu,
                    bias=self.cbcol.ap[:, i:i + 1]), r=[acc[1], self.cbcol], w=[xc[i]])
        self.dump("xc", xc, T, BF16)
        self.dump("xzlast", xz, 2051, BF16)
        self.dump("cwcol", [self.cwcol], 64, F32)
        self.dump("cbcol", [self.cbcol], ET, F32)

        sp1 = self.F[2]
        dtf = [self.F[3], self.F[3]]
        for m in range(ET):
            p = self.PA if m % 2 == 0 else self.PB
            for k in range(ET):
                wb = self._wtile(self.i_dtw.ap[k * 128:(k + 1) * 128,
                                               m * 128:(m + 1) * 128], wi)
                wi += 1
                for f in range(4):
                    em.emit("t", lambda t, pp=p, w=wb, kk=k, ff=f: t.matmul(
                        pp.ap[:, ff * 512:(ff + 1) * 512], w.ap,
                        xc[kk].ap[:, ff * 512:(ff + 1) * 512],
                        start=(kk == 0), stop=(kk == ET - 1)),
                        r=[wb, xc[k]],
                        w=([p] if (k == ET - 1 and f == 3) else []),
                        wfirst=([p] if (k == 0 and f == 0) else []),
                        inc=True)
            d = dtf[m % 2]
            sp2 = self.F[4]
            em.emit("s", lambda s, pp=p, i=m: s.activation(
                sp1.ap, pp.ap, AF.Exp, bias=self.dbcol.ap[:, i:i + 1]),
                r=[p, self.dbcol], w=[sp1])
            em.emit("s", lambda s, sp2=sp2: s.activation(sp2.ap, sp1.ap, AF.Ln,
                    bias=1.0), r=[sp1], w=[sp2])
            em.emit("v", lambda v, d=d, sp2=sp2: v.tensor_scalar(
                d.ap, sp2.ap, 0.001, 0.1, ALU.max, ALU.min), r=[sp2], w=[d])
            em.emit("q", lambda e, d=d, i=m: e.dma_start(self.d_dt.ap[i], d.ap),
                    r=[d], w=[self.d_dt])
            ub = zbuf
            em.emit("v", lambda v, d=d, u=ub, i=m: v.tensor_tensor(
                u.ap[:, :T], d.ap, xc[i].ap[:, :T], ALU.mult),
                r=[d, xc[m]], w=[ub])
            em.emit("q", lambda e, u=ub, i=m: e.dma_start(self.d_u.ap[i],
                    u.ap[:, :T]), r=[ub], w=[self.d_u])
        if "sp" in self.dbg_names:
            pm15 = self.PB
            em.emit("v", lambda v: v.tensor_copy(self.F[1].ap, pm15.ap),
                    r=[pm15], w=[self.F[1]])
        self.dump("dt", dtf, T, F32)
        self.dump("dbcol", [self.dbcol], ET, F32)
        self.dump("sp", [self.F[2], self.F[4], self.F[1]], T, F32)

        p = self.PA
        for k in range(ET):
            wb = self._wtile(self.i_bcw.ap[k * 128:(k + 1) * 128, :], wi)
            wi += 1
            for f in range(4):
                em.emit("t", lambda t, pp=p, w=wb, kk=k, ff=f: t.matmul(
                    pp.ap[:, ff * 512:(ff + 1) * 512], w.ap,
                    xc[kk].ap[:, ff * 512:(ff + 1) * 512],
                    start=(kk == 0), stop=(kk == ET - 1)),
                    r=[wb, xc[k]],
                    w=([p] if (k == ET - 1 and f == 3) else []),
                    wfirst=([p] if (k == 0 and f == 0) else []),
                    inc=True)
        em.emit("s", lambda s: s.copy(self.bcsb.ap, p.ap), r=[p], w=[self.bcsb])
        em.emit("q", lambda e: e.dma_start(self.d_bc.ap, self.bcsb.ap),
                r=[self.bcsb], w=[self.d_bc])
        self.dump("bc", [self.bcsb], T, BF16)

        em.barrier()
        self._precast_weights()

    def _bcast_cols(self, colbuf, out, half=False):
        em = self.em
        p = self.PA
        em.emit("t", lambda t: t.transpose(p.ap[:TT, :128], colbuf.ap,
                self.id32.ap), r=[colbuf, self.id32], w=[p])
        em.emit("s", lambda s: s.copy(self.rowT.ap, p.ap[:TT, :128]),
                r=[p], w=[self.rowT])
        rf = self.rowflat
        em.emit("q", lambda e: e.dma_start(
            rf.ap.rearrange("p (a b) -> p a b", b=128)[:, :TT, :],
            self.rowT.ap), r=[self.rowT], w=[rf])
        p2 = self.PB
        for tq in range(4):
            em.emit("t", lambda t, q=tq, pp=p2: t.matmul(
                pp.ap[:, :512], self.ones32.ap,
                rf.ap[0:1, q * 512:(q + 1) * 512], start=True, stop=True),
                r=[self.ones32, rf], w=[p2])
            em.emit("s", lambda s, pp=p2, q=tq: s.copy(
                out.ap[:, q * 512:(q + 1) * 512], pp.ap[:, :512]),
                r=[p2], w=[out])

    def _precast_weights(self):
        em = self.em
        st = [self.view(self.F[2], self.F[2].ap[:, :1024], "pcs0"),
              self.view(self.F[2], self.F[2].ap[:, 1024:2048], "pcs1")]
        ob = [self.view(self.B[8], self.B[8].ap[:, :1024], "pcb0"),
              self.view(self.B[9], self.B[9].ap[:, :1024], "pcb1")]
        idx = 0

        def cast_store(src_ap, dst_ap, dstbuf, width, scale=None, reshape_c=None):
            nonlocal idx
            s_ = st[idx % 2]
            o_ = ob[idx % 2]
            idx += 1
            em.emit("g", lambda e, s=src_ap, d=s_: e.dma_start(
                d.ap[:, :width], s), w=[s_], dma=True)
            if scale is None:
                em.emit("v", lambda v, s=s_, o=o_: v.tensor_copy(
                    o.ap[:, :width], s.ap[:, :width]), r=[s_], w=[o_])
            else:
                col, cbuf = scale
                em.emit("v", lambda v, s=s_, o=o_, c=col: v.tensor_scalar(
                    o.ap[:, :width], s.ap[:, :width], c, None, ALU.mult),
                    r=[s_, cbuf], w=[o_])
            src_out = o_.ap[:, :width]
            if reshape_c is not None:
                src_out = src_out.rearrange("p (a c) -> p a c", c=reshape_c)
            em.emit("q", lambda e, d=dst_ap, s=src_out: e.dma_start(
                d, s), r=[o_], w=[dstbuf])

        for k in range(ET):
            dst = self.d_opw16.ap[k].rearrange("a p c -> p a c")
            cast_store(self.i_opw.ap[k * 128:(k + 1) * 128, :], dst,
                       self.d_opw16, D, reshape_c=128)
        for e_ in range(NE):
            for k in range(KD):
                cast_store(self.i_w1.ap[e_, k * 128:(k + 1) * 128, :],
                           self.d_w116.ap[e_, k], self.d_w116, DE)
            for k in range(4):
                cast_store(self.i_w2.ap[e_, k * 128:(k + 1) * 128, :],
                           self.d_w216.ap[e_, k], self.d_w216, D)
        for gi, wsrc in ((0, self.i_gw), (1, self.i_uw)):
            for k in range(KD):
                for h in range(2):
                    cast_store(wsrc.ap[k * 128:(k + 1) * 128,
                                       h * 1024:(h + 1) * 1024],
                               self.d_gud16.ap[gi, k][:, h * 1024:(h + 1) * 1024],
                               self.d_gud16, 1024,
                               scale=(self.fwcol.ap[:, k:k + 1], self.fwcol))
        for k in range(ET):
            cast_store(self.i_dw.ap[k * 128:(k + 1) * 128, :],
                       self.d_dw16.ap[k], self.d_dw16, D)

    def _p25_routing(self):
        em = self.em
        cf, wrf, cnb = self.F[0], self.F[1], self.F[2]
        em.emit("q", lambda e: e.dma_start(cf.ap[:NE, :D], self.i_cent.ap), w=[cf])
        em.emit("q", lambda e: e.dma_start(wrf.ap[:NE, :D], self.i_wroute.ap),
                w=[wrf])
        cn2 = self.view(self.scol, self.scol.ap[:NE, 192:196], "cn2")
        scr8 = self.view(self.F[3], self.F[3].ap[:NE, :D], "scr8")
        em.emit("s", lambda s: s.activation(scr8.ap, cf.ap[:NE, :D], AF.Square,
                accum_out=cn2.ap[:, 0:1]), r=[cf], w=[scr8, cn2])
        em.emit("s", lambda s: s.activation(cn2.ap[:, 1:2], cn2.ap[:, 0:1],
                AF.Sqrt), r=[cn2], w=[cn2])
        em.emit("v", lambda v: v.tensor_scalar(cn2.ap[:, 1:2], cn2.ap[:, 1:2],
                self.consts.ap[:NE, 4:5], None, ALU.max),
                r=[cn2, self.consts], w=[cn2])
        em.emit("v", lambda v: v.reciprocal(cn2.ap[:, 2:3], cn2.ap[:, 1:2]),
                r=[cn2], w=[cn2])
        em.emit("v", lambda v: v.tensor_scalar(cnb.ap[:NE, :D], cf.ap[:NE, :D],
                cn2.ap[:, 2:3], None, ALU.mult), r=[cf, cn2], w=[cnb])
        cnT = self.view(self.B[10], self.B[10].ap[:, :KD * NE], "cnT")
        wrT = self.view(self.B[11], self.B[11].ap[:, :KD * NE], "wrT")
        for (srcb, dstb) in ((cnb, cnT), (wrf, wrT)):
            for k in range(KD):
                em.emit("t", lambda t, s=srcb, kk=k: t.transpose(
                    self.PA.ap[:, :NE], s.ap[:NE, kk * 128:(kk + 1) * 128],
                    self.id32.ap[:NE, :NE]), r=[srcb, self.id32], w=[self.PA])
                em.emit("s", lambda s, d=dstb, kk=k: s.copy(
                    d.ap[:, kk * NE:(kk + 1) * NE], self.PA.ap[:, :NE]),
                    r=[self.PA], w=[dstb])
        for (stat, pp) in ((cnT, self.PA), (wrT, self.PB)):
            for f in range(4):
                for k in range(KD):
                    em.emit("t", lambda t, st=stat, kk=k, ff=f, p_=pp: t.matmul(
                        p_.ap[:NE, ff * 512:(ff + 1) * 512],
                        st.ap[:, kk * NE:(kk + 1) * NE],
                        self.xn[kk].ap[:, ff * 512:(ff + 1) * 512],
                        start=(kk == 0), stop=(kk == KD - 1)),
                        r=[stat, self.xn[k]],
                        w=([pp] if (k == KD - 1 and f == 3) else []),
                        wfirst=([pp] if (k == 0 and f == 0) else []),
                        inc=(f == 3))
        csb, rsb = self.F[0], self.F[1]
        em.emit("v", lambda v: v.tensor_copy(csb.ap[:NE, :T],
                self.PA.ap[:NE, :T]), r=[self.PA], w=[csb])
        em.emit("v", lambda v: v.tensor_copy(rsb.ap[:NE, :T],
                self.PB.ap[:NE, :T]), r=[self.PB], w=[rsb])
        costm, routm = self.F[3], self.F[4]
        for (srcb, dstb) in ((csb, costm), (rsb, routm)):
            for tt in range(TT):
                em.emit("t", lambda t, s=srcb, i=tt: t.transpose(
                    self.PA.ap[:, :NE], s.ap[:NE, i * 128:(i + 1) * 128],
                    self.id32.ap[:NE, :NE]), r=[srcb, self.id32], w=[self.PA])
                em.emit("s", lambda s, d=dstb, i=tt: s.copy(
                    d.ap[:, i * NE:(i + 1) * NE], self.PA.ap[:, :NE]),
                    r=[self.PA], w=[dstb])
        tn = self.view(self.scol, self.scol.ap[:, 196:228], "tnorm")
        em.emit("s", lambda s: s.activation(tn.ap[:, 0:TT],
                self.rms_sumsq.ap, AF.Sqrt), r=[self.rms_sumsq], w=[tn])
        em.emit("v", lambda v: v.tensor_scalar(tn.ap[:, 0:TT], tn.ap[:, 0:TT],
                self.consts.ap[:, 4:5], None, ALU.max),
                r=[tn, self.consts], w=[tn])
        em.emit("v", lambda v: v.reciprocal(tn.ap[:, TT:2 * TT],
                tn.ap[:, 0:TT]), r=[tn], w=[tn])
        gtm = self.F[2]
        sc = self.view(self.scol, self.scol.ap[:, 228:240], "tksc")
        msk = self.view(self.scol, self.scol.ap[:, 240:248], "tkmsk")
        for tt in range(TT):
            ls = costm.ap[:, tt * NE:(tt + 1) * NE]
            gs = gtm.ap[:, tt * NE:(tt + 1) * NE]
            em.emit("v", lambda v, ls=ls, i=tt: v.scalar_tensor_tensor(
                ls, ls, tn.ap[:, TT + i:TT + i + 1],
                routm.ap[:, i * NE:(i + 1) * NE], ALU.mult, ALU.add),
                r=[costm, tn, routm], w=[costm])
            em.emit("v", lambda v, ls=ls: v.reduce_max(sc.ap[:, 0:1], ls,
                    axis=AX), r=[costm], w=[sc])
            em.emit("v", lambda v, ls=ls: v.tensor_scalar(
                msk.ap, ls, sc.ap[:, 0:1], None, ALU.is_equal),
                r=[costm, sc], w=[msk])
            em.emit("v", lambda v, ls=ls: v.scalar_tensor_tensor(
                msk.ap, msk.ap, -1e30, ls, ALU.mult, ALU.add),
                r=[msk, costm], w=[msk])
            em.emit("v", lambda v: v.reduce_max(sc.ap[:, 1:2], msk.ap,
                    axis=AX), r=[msk], w=[sc])
            em.emit("v", lambda v: v.tensor_scalar(sc.ap[:, 2:3],
                    sc.ap[:, 0:1], -1.0, None, ALU.mult), r=[sc], w=[sc])
            em.emit("s", lambda s, ls=ls, gs=gs: s.activation(
                gs, ls, AF.Exp, bias=sc.ap[:, 2:3]), r=[costm, sc], w=[gtm])
            em.emit("v", lambda v, ls=ls: v.tensor_scalar(
                msk.ap, ls, sc.ap[:, 1:2], None, ALU.is_ge),
                r=[costm, sc], w=[msk])
            em.emit("v", lambda v, gs=gs: v.tensor_tensor(gs, gs, msk.ap,
                    ALU.mult), r=[gtm, msk], w=[gtm])
            em.emit("v", lambda v, gs=gs: v.reduce_sum(sc.ap[:, 3:4], gs,
                    axis=AX), r=[gtm], w=[sc])
            em.emit("v", lambda v: v.reciprocal(sc.ap[:, 4:5], sc.ap[:, 3:4]),
                    r=[sc], w=[sc])
            em.emit("v", lambda v, gs=gs: v.tensor_scalar(gs, gs,
                    sc.ap[:, 4:5], None, ALU.mult), r=[gtm, sc], w=[gtm])
        for tt in range(TT):
            em.emit("t", lambda t, i=tt: t.transpose(
                self.PA.ap[:NE, :128], gtm.ap[:, i * NE:(i + 1) * NE],
                self.id32.ap), r=[gtm, self.id32], w=[self.PA])
            em.emit("s", lambda s, i=tt: s.copy(
                self.gatesf.ap[:, i * 128:(i + 1) * 128],
                self.PA.ap[:NE, :128]), r=[self.PA], w=[self.gatesf])
        em.emit("q", lambda e: e.dma_start(self.d_g.ap, self.gatesf.ap),
                r=[self.gatesf], w=[self.d_g])
        self.dump("gatesf", [self.view(self.gatesf, self.gatesf.ap, "g")], T, BF16)

    def _p3_scan(self):
        em = self.em
        dA = [self.F[0], self.F[1]]
        dtq = [self.F[2], self.F[2]]
        inp = [self.B[8], self.B[9]]
        hb = [self.B[10], self.B[11]]
        hc = [self.B[12], self.B[13]]
        uq = [self.B[14], self.B[14]]
        y2 = self.B[16:20]
        bbg = [self.B[21], self.B[22]]
        cbg = [self.B[23], self.B[24]]
        heb = [self.B[25], self.B[25]]
        gb = self.view(self.B[20], self.B[20].ap[:, :TC], "gb")
        xpb = self.view(self.B[20], self.B[20].ap[:, 512:1024], "xpb")
        mb = self.view(self.B[20], self.B[20].ap[:, 1024:1536], "mb")
        hetmp = self.view(self.B[20], self.B[20].ap[:, 1536:2048], "hetmp")
        self.moe_tmp = (mb, hetmp)
        ypsum = self.PA
        bcp = [self.view(self.PB, self.PB.ap[:, 0:512], "bcp0"),
               self.view(self.PB, self.PB.ap[:, 512:1024], "bcp1")]
        auxp = [self.view(self.PB, self.PB.ap[:, 1024:1536], "aux0"),
                self.view(self.PB, self.PB.ap[:, 1536:2048], "aux1")]
        for i in range(2):
            em.emit("g", lambda g, b=self.hstate[i], i=i: g.memset(b.ap, 0.0),
                    w=[self.hstate[i]])

        NGRP = NST // NG

        def stage_rows(c, ng):
            stg = self.brow[ng % 2]
            s4 = stg.ap.rearrange("p (h a b) -> p h a b", h=2, b=TC)
            src4 = bass.AP(self.d_bc.ap.tensor,
                           ng * NG * T + c * TC,
                           [[NST * T, 2], [T, NG], [1, TC]]) \
                if False else None
            em.emit("q", lambda e, n0=ng * NG, cc=c, s4=s4: e.dma_start(
                s4[:, 0, :, :],
                self.d_bc.ap[n0:n0 + NG, cc * TC:(cc + 1) * TC]),
                r=[self.d_bc], w=[stg])
            em.emit("q", lambda e, n0=ng * NG, cc=c, s4=s4: e.dma_start(
                s4[:, 1, :, :],
                self.d_bc.ap[NST + n0:NST + n0 + NG, cc * TC:(cc + 1) * TC]),
                r=[self.d_bc], w=[stg])

        def prebuild(c, ng, bb, cb):
            stg = self.brow[ng % 2]
            s3 = stg.ap.rearrange("p (a b) -> p a b", b=TC)
            for jn in range(NG):
                for (half, dst, pp) in ((0, bb, bcp[0]), (1, cb, bcp[1])):
                    em.emit("t", lambda t, p_=pp, j_=jn, h=half, s3=s3: t.matmul(
                        p_.ap, self.ones16.ap,
                        s3[0:1, h * NG + j_, :], start=True, stop=True),
                        r=[self.ones16, stg], w=[pp])
                    em.emit("s", lambda s, p_=pp, d=dst, j_=jn: s.copy(
                        d.ap[:, j_ * TC:(j_ + 1) * TC], p_.ap),
                        r=[pp], w=[dst])

        moe_done = {c: 0 for c in range(NCH)}

        for c in range(NCH):
            g3 = self.grow.ap.rearrange("p (a b) -> p a b", b=TC)
            em.emit("q", lambda e, cc=c: e.dma_start(
                g3[:, :, :], self.d_g.ap[:, cc * TC:(cc + 1) * TC]),
                r=[self.d_g], w=[self.grow])
            for d in range(KD):
                pp = auxp[d % 2]
                em.emit("t", lambda t, p_=pp, d_=d, cc=c: t.matmul(
                    p_.ap, self.b2sb.ap[:, d_ * 128:(d_ + 1) * 128],
                    self.gatesf.ap[:, cc * TC:(cc + 1) * TC],
                    start=True, stop=True), r=[self.b2sb, self.gatesf], w=[pp])
                em.emit("s", lambda s, p_=pp: s.copy(mb.ap, p_.ap),
                        r=[pp], w=[mb])
                em.emit("g", lambda e, d_=d, cc=c: e.dma_start(
                    self.d_moe.ap[d_][:, cc * TC:(cc + 1) * TC], mb.ap),
                    r=[mb], w=[self.d_moev[d]], dma=True)
            hs_r = self.hstate[c % 2]
            hs_w = self.hstate[(c + 1) % 2]
            hr3 = hs_r.ap.rearrange("p (e n) -> p e n", n=NST)
            hw3 = hs_w.ap.rearrange("p (e n) -> p e n", n=NST)
            for qd in range(4):
                dtb_ = dtq[qd % 2]
                ub_ = uq[qd % 2]
                if qd == 0:
                    em.emit("q", lambda e, d=dtb_, cc=c: e.dma_start(
                        d.ap[:, 0:512],
                        self.d_dt.ap[0][:, cc * TC:(cc + 1) * TC]),
                        r=[self.d_dt], w=[dtb_])
                for j in range(4):
                    et = qd * 4 + j
                    em.emit("q", lambda e, d=ub_, i=et, jj=j, cc=c: e.dma_start(
                        d.ap[:, jj * 512:(jj + 1) * 512],
                        self.d_u.ap[i][:, cc * TC:(cc + 1) * TC]),
                        r=[self.d_u], w=[ub_])
                stage_rows(c, 0)
                stage_rows(c, 1)
                prebuild(c, 0, bbg[0], cbg[0])
                for ng in range(NGRP):
                    if ng + 2 < NGRP:
                        stage_rows(c, ng + 2)
                    bb = bbg[ng % 2]
                    cb = cbg[ng % 2]
                    for jn in range(NG):
                        n = ng * NG + jn
                        ib = n % 2
                        em.emit("s", lambda s, d=dA[ib], src=dtb_, n_=n: s.activation(
                            d.ap[:, 0:512], src.ap[:, 0:512], AF.Exp,
                            scale=self.asb.ap[:, n_:n_ + 1]),
                            r=[dtb_, self.asb], w=[dA[ib]])
                        bbr = bb.ap[:, jn * TC:(jn + 1) * TC][:, None, :] \
                            .broadcast_to([128, 4, TC])
                        uv = ub_.ap[:, :2048].rearrange("p (a b) -> p a b", b=TC)
                        iv = inp[ib].ap[:, :2048].rearrange("p (a b) -> p a b", b=TC)
                        em.emit("v", lambda v, iv=iv, uv=uv, bbr=bbr: v.tensor_tensor(
                            iv, uv, bbr, ALU.mult), r=[ub_, bb], w=[inp[ib]])
                        for j in range(4):
                            et = qd * 4 + j
                            em.emit("v", lambda v, ib=ib, j_=j, e_=et, n_=n, hr3=hr3: v.tensor_tensor_scan(
                                hb[ib].ap[:, j_ * 512:(j_ + 1) * 512],
                                dA[ib].ap[:, 0:512],
                                inp[ib].ap[:, j_ * 512:(j_ + 1) * 512],
                                hr3[:, e_:e_ + 1, n_:n_ + 1], ALU.mult, ALU.add),
                                r=[dA[ib], inp[ib], hs_r],
                                w=([hb[ib]] if j == 3 else []), inc=(j == 3))
                        em.emit("v", lambda v, ib=ib, q_=qd, n_=n, hw3=hw3: v.tensor_copy(
                            hw3[:, 4 * q_:4 * q_ + 4, n_:n_ + 1],
                            hb[ib].ap[:, :2048].rearrange(
                                "p (a b) -> p a b", b=512)[:, :, 511:512]),
                            r=[hb[ib]], w=[hs_w])
                        cbr = cb.ap[:, jn * TC:(jn + 1) * TC][:, None, :] \
                            .broadcast_to([128, 4, TC])
                        hv = hb[ib].ap[:, :2048].rearrange("p (a b) -> p a b", b=TC)
                        hcv = hc[ib].ap[:, :2048].rearrange("p (a b) -> p a b", b=TC)
                        em.emit("v", lambda v, hcv=hcv, hv=hv, cbr=cbr, ib=ib: v.tensor_tensor(
                            hcv, hv, cbr, ALU.mult), r=[hb[ib], cb], w=[hc[ib]])
                        for j in range(4):
                            last = (n == NST - 1)
                            em.emit("t", lambda t, ib=ib, j_=j, n_=n: t.matmul(
                                ypsum.ap[:, j_ * 512:(j_ + 1) * 512], self.id16.ap,
                                hc[ib].ap[:, j_ * 512:(j_ + 1) * 512],
                                start=(n_ == 0), stop=(n_ == NST - 1)),
                                r=[self.id16, hc[ib]],
                                w=([ypsum] if (last and j == 3) else []),
                                wfirst=([ypsum] if (n == 0 and j == 0) else []),
                                inc=(j == 3))
                    if ng + 1 < NGRP:
                        prebuild(c, ng + 1, bbg[(ng + 1) % 2], cbg[(ng + 1) % 2])
                    if ng % 8 == 3 and moe_done[c] < 2 * (qd + 1):
                        self._moe_stage(moe_done[c])
                    if ng % 8 == 7 and moe_done[c] < 2 * (qd + 1):
                        ex = moe_done[c]
                        moe_done[c] += 1
                        self._moe_expert(c, ex, heb, gb, auxp)
                zqv = [gb, xpb, mb, hetmp]
                for j in range(4):
                    et = qd * 4 + j
                    em.emit("q", lambda e, i=et, jj=j, cc=c: e.dma_start(
                        self.B[20].ap[:, jj * 512:(jj + 1) * 512],
                        self.d_z.ap[i][:, cc * TC:(cc + 1) * TC]),
                        r=[self.d_z], w=[zqv[j]])
                em.emit("v", lambda v, q_=qd: v.scalar_tensor_tensor(
                    y2[q_].ap[:, :2048], ypsum.ap, 1.0, self.B[20].ap[:, :2048],
                    ALU.mult, ALU.mult), r=[ypsum] + zqv, w=[y2[qd]])
            while moe_done[c] < NE:
                ex = moe_done[c]
                moe_done[c] += 1
                self._moe_stage(ex)
                self._moe_expert(c, ex, heb, gb, auxp)
            if c == 0:
                self.dump("y2c0", y2, 2048, BF16)
            wst = self.B[15]
            w3 = wst.ap[:, :2048].rearrange("p (a b) -> p a b", b=128)
            for d in range(KD):
                pp = auxp[d % 2]
                em.emit("g", lambda e, d_=d, w3=w3: e.dma_start(
                    w3, self.d_opw16.ap[:, d_].transpose([1, 0, 2])), r=[self.d_opw16],
                    w=[wst], dma=True)
                for k in range(ET):
                    em.emit("t", lambda t, p_=pp, kk=k, w3=w3: t.matmul(
                        p_.ap, w3[:, kk, :],
                        y2[kk // 4].ap[:, (kk % 4) * 512:(kk % 4 + 1) * 512],
                        start=(kk == 0), stop=(kk == ET - 1)),
                        r=[wst, y2[k // 4]],
                        w=([pp] if k == ET - 1 else []),
                        wfirst=([pp] if k == 0 else []), inc=True)
                em.emit("v", lambda v, p_=pp, d_=d, cc=c: v.scalar_tensor_tensor(
                    xpb.ap, p_.ap, 1.0,
                    self.xn[d_].ap[:, cc * TC:(cc + 1) * TC], ALU.mult, ALU.add),
                    r=[pp, self.xn[d]], w=[xpb])
                em.emit("q", lambda e, d_=d, cc=c: e.dma_start(
                    self.d_xp.ap[d_][:, cc * TC:(cc + 1) * TC], xpb.ap),
                    r=[xpb], w=[self.d_xpv[d]])

    def _moe_stage(self, ex):
        em = self.em
        w1st, w2st = self.F[3], self.F[4]
        w1b = w1st.ap.bitcast(BF16).rearrange("p (a b) -> p a b", b=512)
        w2b = w2st.ap.bitcast(BF16).rearrange("p (a b) -> p a b", b=1024)
        em.emit("g", lambda e, e_=ex, w1b=w1b: e.dma_start(
            w1b, self.d_w116.ap[e_].transpose([1, 0, 2])),
            r=[self.d_w116], w=[w1st], dma=True)
        em.emit("g", lambda e, e_=ex, w2b=w2b: e.dma_start(
            w2b, self.d_w216.ap[e_].transpose([1, 0, 2])),
            r=[self.d_w216], w=[w2st], dma=True)

    def _moe_expert(self, c, ex, heb, gb, auxp):
        em = self.em
        he = heb[ex % 2]
        w1st, w2st = self.F[3], self.F[4]
        w1b = w1st.ap.bitcast(BF16).rearrange("p (a b) -> p a b", b=512)
        w2b = w2st.ap.bitcast(BF16).rearrange("p (a b) -> p a b", b=1024)
        g3 = self.grow.ap.rearrange("p (a b) -> p a b", b=TC)
        em.emit("t", lambda t, e_=ex: t.matmul(
            auxp[0].ap, self.ones16.ap, g3[0:1, e_, :],
            start=True, stop=True), r=[self.ones16, self.grow], w=[auxp[0]])
        em.emit("s", lambda s: s.copy(gb.ap, auxp[0].ap), r=[auxp[0]], w=[gb])
        for m in range(4):
            pq = auxp[1]
            for k in range(KD):
                em.emit("t", lambda t, p_=pq, k=k, mm=m, cc=c, w1b=w1b: t.matmul(
                    p_.ap, w1b[:, k, mm * 128:(mm + 1) * 128],
                    self.xn[k].ap[:, cc * TC:(cc + 1) * TC],
                    start=(k == 0), stop=(k == KD - 1)),
                    r=[w1st, self.xn[k]], w=([pq] if k == KD - 1 else []),
                    wfirst=([pq] if k == 0 else []), inc=True)
            hetmp = self.moe_tmp[1]
            em.emit("s", lambda s, p_=pq, e_=ex, mm=m, h_=hetmp: s.activation(
                h_.ap, p_.ap, AF.Gelu,
                bias=self.b1col.ap[:, e_ * 4 + mm:e_ * 4 + mm + 1]),
                r=[pq, self.b1col], w=[hetmp])
            em.emit("v", lambda v, mm=m, h_=hetmp: v.tensor_tensor(
                he.ap[:, mm * TC:(mm + 1) * TC], h_.ap, gb.ap, ALU.mult),
                r=[hetmp, gb], w=[he])
        for d in range(KD):
            pq = auxp[1]
            for k in range(4):
                em.emit("t", lambda t, p_=pq, k=k, d_=d, w2b=w2b: t.matmul(
                    p_.ap, w2b[:, k, d_ * 128:(d_ + 1) * 128],
                    he.ap[:, k * TC:(k + 1) * TC],
                    start=(k == 0), stop=(k == 3)),
                    r=[w2st, he], w=([pq] if k == 3 else []),
                    wfirst=([pq] if k == 0 else []), inc=True)
            mb = self.moe_tmp[0]
            em.emit("s", lambda s, p_=pq, m_=mb: s.copy(m_.ap, p_.ap),
                    r=[pq], w=[mb])
            em.emit("g", lambda e, d_=d, cc=c, m_=mb: e.dma_start(
                self.d_moe.ap[d_][:, cc * TC:(cc + 1) * TC], m_.ap,
                accum_op=ALU.add), r=[mb], w=[self.d_moev[d]], dma=True)

    def _p4_ffn_final(self):
        for hf in range(2):
            self._p4_half(hf)
            self.em.barrier()
        if P4SUB < 8:
            em = self.em
            for tt in range(TT):
                em.emit("q", lambda e, i=tt: e.dma_start(
                    self.o_out.ap[i * 128:(i + 1) * 128, :],
                    self.F[0].ap[:, :D]), r=[self.F[0]], w=[self.o_out])
            em.barrier()

    def _p4_half(self, hf):
        em = self.em
        TH = T // 2
        t0 = hf * TH
        xp = self.B[0:8]
        pre = self.B[8:16]
        fbs = self.B[16:24]

        def fb_ap(m):
            return fbs[m // 2].ap[:, (m % 2) * TH:(m % 2 + 1) * TH]

        def fb_buf(m):
            return fbs[m // 2]

        gtmp = self.view(self.B[24], self.B[24].ap[:, :TH], "gtmp")
        u2t = self.view(self.B[24], self.B[24].ap[:, TH:2 * TH], "u2t")
        rb = self.view(self.B[25], self.B[25].ap[:, :TH], "rb4")
        sqb = [self.view(self.B[25], self.B[25].ap[:, TH:2 * TH], "sq0")] * 2
        f1b = self.F[1].ap.bitcast(BF16)
        xfb = self.view(self.F[1], f1b[:, :TH], "xfb")
        for k in range(KD):
            em.emit("q", lambda e, i=k, b=xp[k], o=t0: e.dma_start(
                b.ap[:, :TH], self.d_xp.ap[i][:, o:o + TH]),
                r=[self.d_xpv[k]], w=[xp[k]])
        prow = self.view(self.PA, self.PA.ap[0:1, :1024], "prow")
        for k in range(KD):
            sq = sqb[k % 2]
            em.emit("s", lambda s, i=k, q=sq: s.activation(
                q.ap, xp[i].ap[:, :TH], AF.Square), r=[xp[k]], w=[sq])
            for f in range(2):
                em.emit("t", lambda t, kk=k, ff=f, q=sq: t.matmul(
                    prow.ap[:, ff * 512:(ff + 1) * 512], self.onecol16.ap,
                    q.ap[:, ff * 512:(ff + 1) * 512],
                    start=(kk == 0), stop=(kk == KD - 1)),
                    r=[self.onecol16, sq],
                    w=([prow, self.PA] if (k == KD - 1 and f == 1) else []),
                    wfirst=([prow, self.PA] if (k == 0 and f == 0) else []),
                    inc=(f == 1))
        sr = self.srow
        em.emit("v", lambda v: v.tensor_scalar(sr.ap[:, :TH], prow.ap[:, :TH],
                1.0 / D, None, ALU.mult), r=[prow], w=[sr])
        em.emit("s", lambda s: s.activation(sr.ap[:, TH:2 * TH], sr.ap[:, :TH],
                AF.Sqrt, bias=self.consts.ap[:1, 0:1]),
                r=[sr, self.consts], w=[sr])
        em.emit("v", lambda v: v.reciprocal(sr.ap[:, 2 * TH:3 * TH],
                sr.ap[:, TH:2 * TH]), r=[sr], w=[sr])
        if P4SUB < 2:
            return
        em.barrier()
        for f in range(2):
            em.emit("t", lambda t, ff=f: t.matmul(
                self.PA.ap[:, ff * 512:(ff + 1) * 512], self.ones32.ap,
                sr.ap[:, 2 * TH + ff * 512:2 * TH + (ff + 1) * 512],
                start=True, stop=True), r=[self.ones32, sr],
                w=([self.PA] if f == 1 else []),
                wfirst=([self.PA] if f == 0 else []), inc=(f == 1))
        em.emit("s", lambda s: s.copy(rb.ap, self.PA.ap[:, :TH]),
                r=[self.PA], w=[rb])
        for d in range(KD):
            t1 = sqb[0]
            em.emit("q", lambda e, d_=d, b=xfb, o=t0: e.dma_start(
                b.ap, self.d_xf.ap[d_][:, o:o + TH]), r=[self.d_xf], w=[xfb])
            em.emit("v", lambda v, d_=d, b=xfb, t1=t1: v.tensor_tensor(
                t1.ap, b.ap, xp[d_].ap[:, :TH], ALU.add),
                r=[xfb, xp[d]], w=[t1])
            em.emit("q", lambda e, d_=d, b=xfb, o=t0: e.dma_start(
                b.ap, self.d_moe.ap[d_][:, o:o + TH]),
                r=[self.d_moev[d]], w=[xfb])
            em.emit("v", lambda v, d_=d, b=xfb, t1=t1: v.tensor_tensor(
                pre[d_].ap[:, :TH], t1.ap, b.ap, ALU.add),
                r=[xfb, t1], w=[pre[d]])
        if P4SUB < 7:
            return
        gust = self.F[3]
        gub = gust.ap.bitcast(BF16).rearrange("p (a b) -> p a b", b=128)
        for m in range(ET):
            pg, pu = self.PA, self.PB
            em.emit("g", lambda e, mm=m, gub=gub: e.dma_start(
                gub[:, 0:8, :], self.d_gud16.ap[0, :, :, mm * 128:(mm + 1) * 128].transpose([1, 0, 2])),
                r=[self.d_gud16], w=[gust], dma=True)
            em.emit("g", lambda e, mm=m, gub=gub: e.dma_start(
                gub[:, 8:16, :], self.d_gud16.ap[1, :, :, mm * 128:(mm + 1) * 128].transpose([1, 0, 2])),
                r=[self.d_gud16], w=[gust], dma=True)
            for k in range(KD):
                for f in range(2):
                    em.emit("t", lambda t, pp=pg, kk=k, ff=f, gub=gub: t.matmul(
                        pp.ap[:, ff * 512:(ff + 1) * 512], gub[:, kk, :],
                        xp[kk].ap[:, ff * 512:(ff + 1) * 512],
                        start=(kk == 0), stop=(kk == KD - 1)),
                        r=[gust, xp[k]],
                        w=([pg] if (k == KD - 1 and f == 1) else []),
                        wfirst=([pg] if (k == 0 and f == 0) else []),
                        inc=(f == 1))
                    em.emit("t", lambda t, pp=pu, kk=k, ff=f, gub=gub: t.matmul(
                        pp.ap[:, ff * 512:(ff + 1) * 512], gub[:, 8 + kk, :],
                        xp[kk].ap[:, ff * 512:(ff + 1) * 512],
                        start=(kk == 0), stop=(kk == KD - 1)),
                        r=[gust, xp[k]],
                        w=([pu] if (k == KD - 1 and f == 1) else []),
                        wfirst=([pu] if (k == 0 and f == 0) else []),
                        inc=(f == 1))
            em.emit("v", lambda v: v.tensor_tensor(gtmp.ap,
                    self.PA.ap[:, :TH], rb.ap, ALU.mult),
                    r=[self.PA, rb], w=[gtmp])
            em.emit("s", lambda s: s.activation(xfb.ap, gtmp.ap, AF.Silu),
                    r=[gtmp], w=[xfb])
            em.emit("v", lambda v: v.tensor_tensor(u2t.ap, self.PB.ap[:, :TH],
                    rb.ap, ALU.mult), r=[self.PB, rb], w=[u2t])
            em.emit("v", lambda v, i=m: v.tensor_tensor(fb_ap(i), xfb.ap,
                    u2t.ap, ALU.mult), r=[xfb, u2t], w=[fb_buf(m)])
        if P4SUB < 8:
            return
        otm = [self.view(self.F[2], self.F[2].ap[:, :512], "otm0"),
               self.view(self.F[2], self.F[2].ap[:, 512:1024], "otm1")]
        ofm = self.F[0]
        dwst = self.F[4]
        dwb = dwst.ap.bitcast(BF16).rearrange("p (a b) -> p a b", b=128)
        for d in range(KD):
            pp = self.PA if d % 2 == 0 else self.PB
            em.emit("g", lambda e, d_=d, dwb=dwb: e.dma_start(
                dwb[:, 0:16, :], self.d_dw16.ap[:, :, d_ * 128:(d_ + 1) * 128].transpose([1, 0, 2])),
                r=[self.d_dw16], w=[dwst], dma=True)
            for k in range(ET):
                for f in range(2):
                    em.emit("t", lambda t, pp=pp, kk=k, ff=f, dwb=dwb: t.matmul(
                        pp.ap[:, ff * 512:(ff + 1) * 512], dwb[:, kk, :],
                        fb_ap(kk)[:, ff * 512:(ff + 1) * 512],
                        start=(kk == 0), stop=(kk == ET - 1)),
                        r=[dwst, fb_buf(k)],
                        w=([pp] if (k == ET - 1 and f == 1) else []),
                        wfirst=([pp] if (k == 0 and f == 0) else []),
                        inc=(f == 1))
            em.emit("v", lambda v, pp=pp, d_=d: v.scalar_tensor_tensor(
                ofm.ap[:, :TH], pp.ap[:, :TH], 1.0, pre[d_].ap[:, :TH],
                ALU.mult, ALU.add), r=[pp, pre[d]], w=[ofm])
            po = self.PB if d % 2 == 0 else self.PA
            for tq in range(2):
                o = otm[tq % 2]
                for j in range(4):
                    tt = tq * 4 + j
                    em.emit("t", lambda t, pp=po, jj=j, i=tt: t.transpose(
                        pp.ap[:, 1024 + jj * 128:1024 + (jj + 1) * 128],
                        ofm.ap[:, i * 128:(i + 1) * 128], self.id32.ap),
                        r=[ofm, self.id32], w=([po] if j == 3 else []),
                        wfirst=([po] if j == 0 else []), inc=(j == 3))
                em.emit("v", lambda v, pp=po, o=o: v.tensor_copy(
                    o.ap, pp.ap[:, 1024:1536]), r=[po], w=[o])
                for j in range(4):
                    tt = tq * 4 + j
                    em.emit("q", lambda e, o=o, d_=d, i=tt, jj=j, o0=t0: e.dma_start(
                        self.o_out.ap[o0 + i * 128:o0 + (i + 1) * 128,
                                      d_ * 128:(d_ + 1) * 128],
                        o.ap[:, jj * 128:(jj + 1) * 128]),
                        r=[o], w=[self.o_out])


def build_program(dbg=()):
    kb = KB(dbg=dbg)
    nc = kb.build()
    return nc


def _in_maps(inputs):
    f32 = np.float32
    x = np.ascontiguousarray(np.asarray(inputs["L0_attn_x"], dtype=f32))

    def g(k):
        return np.ascontiguousarray(np.asarray(inputs[k], dtype=f32))

    common = {
        "inw": g("L0_attn_in_proj_W"),
        "convw": g("L0_attn_conv_W").reshape(4, E),
        "convb": g("L0_attn_conv_b"),
        "dtw": g("L0_attn_dt_W"),
        "dtb": g("L0_attn_dt_b"),
        "bcw": np.ascontiguousarray(np.concatenate(
            [g("L0_attn_B_W"), g("L0_attn_C_W")], axis=1)),
        "alog": g("L0_attn_A_log").reshape(1, NST),
        "opw": g("L0_attn_out_proj_W"),
        "gw": g("L0_attn_gate_W"),
        "uw": g("L0_attn_up_W"),
        "dw": g("L0_attn_down_W"),
        "normw": g("L0_attn_input_norm_w"),
        "fnormw": g("L0_attn_ffn_norm_w"),
        "w1": g("L0_ffn_w1"),
        "b1": g("L0_ffn_b1"),
        "w2": g("L0_ffn_w2"),
        "b2": g("L0_ffn_b2"),
        "cent": g("L0_ffn_centroids"),
        "wroute": g("L0_ffn_w_route"),
    }
    return [dict(common, x=np.ascontiguousarray(x[i])) for i in range(N_CORES)]


def kernel(**inputs):
    nc = build_program(dbg=DBG)
    res = run_bass_kernel_spmd(nc, _in_maps(inputs), core_ids=list(range(N_CORES)))
    out = np.stack([res.results[i]["out"] for i in range(N_CORES)], axis=0)
    return np.asarray(out, dtype=np.float32)
